# version 19
# baseline (speedup 1.0000x reference)
import numpy as np
import concourse.bass as bass
import concourse.mybir as mybir
from concourse.bass import broadcast_tensor_aps
from concourse.bass_utils import run_bass_kernel_spmd

F32 = mybir.dt.float32
BF16 = mybir.dt.bfloat16
AF = mybir.ActivationFunctionType
ALU = mybir.AluOpType

B, S, E = 8, 4096, 1024
NP = 8
NT = S // 512
NJ = S // 128
NE = E // 128
WINDOW = 16
NEG = -1.0e9
CSHIFT = 3.0
EPS = 1e-5
RC0, RC1, RC2, RC3 = (1.9871303416114157, -0.9757659737364119,
                      0.19554620527220662, -0.012522044838806663)
BST = 8
NB = NJ // BST
SBANKS = (0, 1, 2, 3, 6)

ENGS = ("pe", "act", "dve", "gp", "sp")
LAST_SCHED = None


DUR = {
    "qk": 2.0, "v": 2.2, "qkev": 0.7, "vev": 0.6,
    "sc": 0.7, "mk": 0.35, "exp": 1.05, "cx": 1.9, "acc": 1.3,
    "nq_st1": 7.0, "nq_hl": 2.6, "bc": 0.5, "nq_st3": 1.3,
    "nq_st4": 0.8,
    "proj": 1.9, "z": 0.7, "sq": 0.65, "s1": 3.6, "s2": 1.9,
    "rowsa": 0.65, "rowsb": 13.0, "mb": 0.5, "rs": 0.6,
    "mbe": 0.65, "rse": 0.65, "y1": 1.7, "out": 1.3,
    "h1": 1.9, "h2": 1.9, "h3": 1.9, "h1e": 0.65, "h2e": 0.65,
    "tanh": 0.8, "z2": 0.65, "ms": 0.3, "vones": 0.3, "fin": 0.1,
    "warm": 0.06,
}
DMA_TAGS = {"ldxf": 2048, "ldb": 4, "ldw": 192, "qkst": 128, "vst": 128,
            "ldk": 1024, "ldv": 512, "ldq": 128, "stc": 64, "ldc": 512,
            "ldxr": 256, "sto": 256, "ldwo": 512, "ldwf": 2048,
            "dnst": 4, "dnbc": 256,
            "lda": 2048}


class Sched:

    def __init__(self):
        self.cnt = {}
        self.ops = {e: [] for e in ENGS}
        self.lanes = []
        self.semmap = {}

    def lane(self):
        ln = Lane(self)
        self.lanes.append(ln)
        return ln

    def v(self, sem):
        return self.cnt.get(sem, 0)

    def merge(self):
        heads = {e: [0] * len(self.lanes) for e in ENGS}
        etime = {e: 0.0 for e in ENGS}
        semt = {}

        def semtime(name, val):
            lst = semt.get(name)
            if lst is None or not lst or lst[-1][0] < val:
                return None
            from bisect import bisect_left
            vals = [x[0] for x in lst]
            i = bisect_left(vals, val)
            return lst[i][1]

        total = sum(len(ln.q[e]) for ln in self.lanes for e in ENGS)
        done = 0
        while done < total:
            progress = False
            best = None
            for e in ENGS:
                for li, ln in enumerate(self.lanes):
                    hi = heads[e][li]
                    if hi >= len(ln.q[e]):
                        continue
                    waits, fn, inc, amt, tag = ln.q[e][hi]
                    t = etime[e]
                    ok = True
                    for (h, vv) in waits:
                        st_ = semtime(self.semmap[id(h)], vv)
                        if st_ is None:
                            ok = False
                            break
                        t = max(t, st_)
                    if not ok:
                        continue
                    if best is None or t < best[0]:
                        best = (t, e, li)
            if best is None:
                raise RuntimeError("merge deadlock: no selectable op")
            t, e, li = best
            ln = self.lanes[li]
            waits, fn, inc, amt, tag = ln.q[e][heads[e][li]]
            heads[e][li] += 1
            done += 1
            progress = True
            base = tag.rstrip("0123456789")
            if base in DMA_TAGS:
                dur = 1.2 + DMA_TAGS[base] * 2 / 400.0
                issue = 0.25
                fin_t = max(t, etime[e]) + issue + dur
                etime[e] = max(t, etime[e]) + issue
            else:
                dur = DUR.get(base, DUR.get(tag, 0.6))
                fin_t = max(t, etime[e]) + 0.12 + dur
                etime[e] = fin_t
            self.ops[e].append((waits, fn, inc, amt, tag))
            if inc is not None:
                nv = self.cnt2.get(inc, 0) + amt
                self.cnt2[inc] = nv
                semt.setdefault(inc, []).append((nv, fin_t))
        self.est_span = max(etime.values())

    def run_merge(self):
        self.cnt2 = {}
        self.merge()


class Lane:
    def __init__(self, sch):
        self.sch = sch
        self.q = {e: [] for e in ENGS}

    def op(self, eng, fn, waits=(), inc=None, amt=1, tag=""):
        val = None
        if inc is not None:
            val = self.sch.cnt.get(inc, 0) + amt
            self.sch.cnt[inc] = val
        self.q[eng].append((tuple(w for w in waits if w[1] is not None
                                  and w[1] > 0), fn, inc, amt, tag))
        return val

    def v(self, sem):
        return self.sch.v(sem)


def mask_windows(it, jc):
    sp = 128 * jc - 512 * it
    if sp == -128:
        return [(0, 128, 16)]
    if 0 <= sp <= 384:
        return [(sp, 0, min(144, 512 - sp))]
    return []


def build_nc(phases=(1, 2, 3)):
    nc = bass.Bass("TRN2", target_bir_lowering=False)

    xT = nc.dram_tensor("xT", [E, S], F32, kind="ExternalInput")
    w_inT = nc.dram_tensor("w_inT", [E, 3 * E], F32, kind="ExternalInput")
    w_outT = nc.dram_tensor("w_outT", [E, E], F32, kind="ExternalInput")
    w_ffn = nc.dram_tensor("w_ffn", [E, E], F32, kind="ExternalInput")
    w_a1 = nc.dram_tensor("w_a1", [E, E], F32, kind="ExternalInput")
    w_a2 = nc.dram_tensor("w_a2", [E, E], F32, kind="ExternalInput")
    bq_d = nc.dram_tensor("bq", [128, NE], F32, kind="ExternalInput")
    bk_d = nc.dram_tensor("bk", [128, NE], F32, kind="ExternalInput")
    bv_d = nc.dram_tensor("bv", [128, NE], F32, kind="ExternalInput")
    bvB_d = nc.dram_tensor("bvB", [128, NE], F32, kind="ExternalInput")
    bo_d = nc.dram_tensor("bo", [128, NE], F32, kind="ExternalInput")
    bf_d = nc.dram_tensor("bf", [128, NE], F32, kind="ExternalInput")
    mk_d = nc.dram_tensor("maskT", [128, 144], BF16, kind="ExternalInput")
    yT = nc.dram_tensor("yT", [E, S], F32, kind="ExternalOutput")
    q_s = nc.dram_tensor("q_s", [E, S], BF16, kind="Internal")
    den_d = nc.dram_tensor("den_d", [2, 1024], F32, kind="Internal")
    k_s = nc.dram_tensor("k_s", [E, S], BF16, kind="Internal")
    v_s = nc.dram_tensor("v_s", [NP, S, 128], BF16, kind="Internal")
    ctx_s = nc.dram_tensor("ctx_s", [E, S], BF16, kind="Internal")

    from contextlib import ExitStack
    st = ExitStack()
    sb = lambda n, sh, dt: st.enter_context(nc.sbuf_tensor(n, sh, dt))

    big = sb("big", [128, 49152], BF16)
    x_fm = big[:, 0:32768]
    wp = [big[:, 32768 + i * 3072:32768 + (i + 1) * 3072] for i in (0, 1)]
    qk_buf = [big[:, 38912 + i * 512:38912 + (i + 1) * 512] for i in range(4)]
    v_buf = [big[:, 40960 + i * 512:40960 + (i + 1) * 512] for i in (0, 1)]
    wout_sb = big[:, 0:8192]
    wslot = [big[:, 8192 + i * 8192:8192 + (i + 1) * 8192] for i in (0, 1)]
    z = big[:, 24576:32768].bitcast(F32)
    y1f = big[:, 32768:40960].bitcast(F32)
    y1b = big[:, 40960:45056]
    h2b = y1b
    h1b = big[:, 45056:49152]
    sq8 = h1b

    k_pair = sb("k_pair", [128, 2 * S], BF16)
    ctx_t = sb("ctx_t", [128, S], BF16)
    v_pair = sb("v_pair", [128, 32 * 130], BF16)
    q_buf = sb("q_buf", [128, 1024], BF16)
    E_sb = sb("E_sb", [128, 14336], BF16)
    acc = sb("acc", [128, 2048], F32)
    Rb_sb = sb("Rb_sb", [128, 1024], F32)
    den_bc = sb("den_bc", [64, 1024], F32)
    stage_sb = sb("stage_sb", [128, 1024], BF16)
    rows3 = sb("rows3", [128, 1536], F32)
    rows3b = sb("rows3b", [128, 1536], BF16)
    mb_sb = sb("mb_sb", [128, 512], F32)
    rb3_sb = sb("rb3_sb", [128, 512], F32)
    x_ring = sb("x_ring", [128, 1024], F32)
    sq = sb("sq", [128, 512], F32)
    ffn_t = sb("ffn_t", [128, 1024], F32)
    out_t = sb("out_t", [128, 1024], F32)
    bq = sb("bq_s", [128, NE], F32)
    bk = sb("bk_s", [128, NE], F32)
    bv = sb("bv_s", [128, NE], F32)
    bvB = sb("bvB_s", [128, NE], F32)
    bo = sb("bo_s", [128, NE], F32)
    bf = sb("bf_s", [128, NE], F32)
    maskT = sb("maskT_s", [128, 144], BF16)
    ones_bf = sb("ones_bf", [128, 128], BF16)
    ones_f32 = sb("ones_f32", [128, 1], F32)
    cexp = sb("cexp", [128, 1], F32)

    P = st.enter_context(nc.psum_tensor("P", [128, 4096], F32))
    Z6, Z7 = 3072, 3584

    sems = {}
    for name in ("LDX", "LDB", "LDW0", "LDW1", "P1", "E1",
                 "STQ0", "STQ1", "STQ2", "STQ3", "STV0", "STV1",
                 "KVK0", "KVK1", "KVV", "KVQ0", "KVQ1",
                 "SC", "MK", "EX", "CX", "ACC", "N1", "N1A", "BC", "N2", "N3",
                 "CS0", "CS1", "LC", "LX0", "LX1",
                 "LWO", "LW0", "LW1", "P3", "V3", "A3",
                 "OS0", "OS1"):
        sems[name] = st.enter_context(nc.semaphore(name))

    sch = Sched()
    sch.semmap = {id(h): n for n, h in sems.items()}
    la = sch.lane()
    lnm = sch.lane()
    lp = sch.lane()
    ls = sch.lane()
    zone7 = [None]

    def W(*pairs):
        return [(sems[s], v) for (s, v) in pairs
                if v is not None and v > 0]

    for ec in range(NE):
        la.op("gp", lambda g, ec=ec: g.dma_start(
            x_fm[:, ec * 4096:(ec + 1) * 4096],
            xT[ec * 128:(ec + 1) * 128, :]), inc="LDX", amt=16, tag="ldxf")
    for dd, t in ((bq_d, bq), (bk_d, bk), (bv_d, bv), (bvB_d, bvB),
                  (bo_d, bo), (bf_d, bf), (mk_d, maskT)):
        la.op("sp", lambda s, dd=dd, t=t: s.dma_start(t[:, :], dd[:, :]),
               inc="LDB", amt=16, tag="ldb")
    la.op("dve", lambda v: v.memset(ones_bf[:, :], 1.0), inc="E1", tag="ms")
    la.op("dve", lambda v: v.memset(ones_f32[:, :], 1.0), inc="E1",
          tag="ms")
    la.op("dve", lambda v: v.memset(cexp[:, :], -CSHIFT), inc="E1",
          tag="ms")

    def vones(v):
        vv = v_pair[:, :].rearrange("p (j c) -> p j c", c=130)
        v.memset(vv[:, :, 64:65], 1.0)
        return v.memset(vv[:, :, 129:130], 1.0)
    la.op("dve", vones, inc="E1", tag="vones")
    LDX_ALL = sch.v("LDX")
    LDB_ALL = sch.v("LDB")
    MS0 = sch.v("E1")

    wp_done = [None] * NP
    pair_p1_end = [None] * NP
    qk_ev, v_ev = [], []
    qk_n = 0
    v_n = 0
    for p in range(NP):
        gate = []
        if p >= 2:
            gate.append(("P1", pair_p1_end[p - 2]))
        for kind in range(3):
            def ldwp(g, p=p, kind=kind):
                out = wp[p % 2][:, :].rearrange(
                    "p (ec k d) -> p ec k d", ec=NE, k=3,
                    d=128)[:, :, kind, :]
                src = w_inT[:, kind * E + p * 128:
                            kind * E + (p + 1) * 128].rearrange(
                    "(ec p) f -> p ec f", p=128)
                return g.dma_start(out, src)
            la.op("gp", ldwp, waits=W(*gate), inc=f"LDW{p % 2}",
                  amt=16, tag="ldw")
        wp_done[p] = 48 * (p // 2 + 1)

        for kind in range(2):
            for tt in range(NT):
                waits = [("LDX", LDX_ALL), (f"LDW{p % 2}", wp_done[p])]
                if qk_n >= 2:
                    waits.append(("E1", qk_ev[qk_n - 2]))
                bank = qk_n % 2
                def mmg(t, p=p, kind=kind, tt=tt, bank=bank):
                    last = None
                    for ec in range(NE):
                        last = t.matmul(
                            P[:, bank * 512:bank * 512 + 512],
                            wp[p % 2][:, ec * 384 + kind * 128:
                                      ec * 384 + kind * 128 + 128],
                            x_fm[:, ec * 4096 + tt * 512:
                                 ec * 4096 + tt * 512 + 512],
                            start=(ec == 0), stop=(ec == NE - 1))
                    return last
                la.op("pe", mmg, waits=W(*waits), inc="P1", tag="qk")
                pv = sch.v("P1")
                ew = [("P1", pv), ("E1", MS0), ("LDB", LDB_ALL)]
                if qk_n >= 4:
                    ew.append((f"STQ{qk_n % 4}", 16 * (qk_n // 4)))
                bias = bq if kind == 0 else bk
                buf = qk_n % 4
                def ev(v, p=p, bank=bank, bias=bias, buf=buf):
                    return v.tensor_scalar(
                        qk_buf[buf][:, :],
                        P[:, bank * 512:bank * 512 + 512],
                        bias[:, p:p + 1], None, ALU.add)
                la.op("dve", ev, waits=W(*ew), inc="E1", tag="qkev")
                qk_ev.append(sch.v("E1"))
                dst = q_s if kind == 0 else k_s
                def stq(s, p=p, tt=tt, buf=buf, dst=dst):
                    return s.dma_start(
                        dst[p * 128:(p + 1) * 128,
                            tt * 512:(tt + 1) * 512],
                        qk_buf[buf][:, :])
                la.op("sp", stq, waits=W(("E1", qk_ev[-1])),
                      inc=f"STQ{qk_n % 4}", amt=16, tag="qkst")
                qk_n += 1

        for jq in range(NJ // 4):
            waits = [("LDX", LDX_ALL), (f"LDW{p % 2}", wp_done[p])]
            if v_n >= 2:
                waits.append(("E1", v_ev[v_n - 2]))
            bank = v_n % 2
            def vmm(t, p=p, jq=jq, bank=bank):
                last = None
                pbase = 1024 + bank * 512
                for sub in range(4):
                    jc = jq * 4 + sub
                    for ec in range(NE):
                        last = t.matmul(
                            P[:, pbase + sub * 128:pbase + sub * 128 + 128],
                            x_fm[:, ec * 4096 + jc * 128:
                                 ec * 4096 + jc * 128 + 128],
                            wp[p % 2][:, ec * 384 + 256:ec * 384 + 384],
                            start=(ec == 0), stop=(ec == NE - 1))
                return last
            la.op("pe", vmm, waits=W(*waits), inc="P1", tag="v")
            pv = sch.v("P1")
            ew = [("P1", pv)]
            if v_n >= 2:
                ew.append((f"STV{v_n % 2}", 16 * (v_n // 2)))
            def vev(v, bank=bank):
                return v.tensor_copy(v_buf[bank][:, :],
                                     P[:, 1024 + bank * 512:
                                       1024 + bank * 512 + 512])
            la.op("dve", vev, waits=W(*ew), inc="E1", tag="vev")
            v_ev.append(sch.v("E1"))
            def stv(s, p=p, jq=jq, bank=bank):
                out = v_s[p, jq * 512:(jq + 1) * 512, :].rearrange(
                    "(c p) d -> p c d", p=128)
                return s.dma_start(out, v_buf[bank][:, :])
            la.op("sp", stv, waits=W(("E1", v_ev[-1])),
                  inc=f"STV{v_n % 2}", amt=16, tag="vst")
            v_n += 1
        pair_p1_end[p] = sch.v("P1")

    ST1_ALL_W = [("STQ0", sch.v("STQ0")), ("STQ1", sch.v("STQ1")),
                 ("STQ2", sch.v("STQ2")), ("STQ3", sch.v("STQ3")),
                 ("STV0", sch.v("STV0")), ("STV1", sch.v("STV1"))]
    E1_ALL = sch.v("E1")
    P1_ALL = sch.v("P1")

    if 2 not in phases:
        def dbg(gq):
            return gq.dma_start(yT[:, 0:2048],
                                q_s[:, :].rearrange("p (a b) -> p a b",
                                                    a=2)[:, 0, :])
        la.op("gp", dbg, waits=W(*ST1_ALL_W), inc="OS0", amt=16)
        def dbg2(gq):
            return gq.dma_start(yT[:, 2048:4096],
                                k_s[:, :].rearrange("p (a b) -> p a b",
                                                    a=2)[:, 0, :])
        la.op("gp", dbg2, waits=W(*ST1_ALL_W), inc="OS0", amt=16)
        la.op("sp", lambda s_: s_.nop(), waits=W(("OS0", sch.v("OS0"))))
        sch.run_merge()
        return _finish(nc, sch, sems, st)

    for ec in range(NE):
        def ldwo(gq, ec=ec):
            return gq.dma_start(wout_sb[:, ec * 1024:(ec + 1) * 1024],
                                w_outT[ec * 128:(ec + 1) * 128, :])
        la.op("gp", ldwo, waits=W(("P1", P1_ALL)), inc="LWO", amt=16,
              tag="ldwo")
    LW_WOUT = sch.v("LWO")

    ln_state = {"out_n": 0}
    zglob = []
    cs_ready = [None] * NT
    proj_done = [None] * NT

    zstate = {"wide": False, "last": [None, None], "rr": 0,
              "first": True}

    def z7use():
        if not zstate["wide"]:
            w = zone7[0]
            return [w] if w is not None else [], Z7
        b = zstate["rr"]
        zstate["rr"] ^= 1
        zstate["cur"] = b
        w = zstate["last"][b]
        ws = [w] if w is not None else []
        if zstate["first"]:
            ws = ws + W(("EX", 4096))
            if b == 1:
                zstate["first"] = False
        return ws, b * 512

    def z7done(marker):
        if not zstate["wide"]:
            zone7[0] = marker
        else:
            zstate["last"][zstate["cur"]] = marker

    def ln_rows_b(v, zc_):
        m = rows3[0:1, 0:512]
        var = rows3[0:1, 512:1024]
        y = rows3[0:1, 1024:1536]
        t1 = mb_sb[0:1, :]
        t2 = rb3_sb[0:1, :]
        v.tensor_scalar(t1, P[0:1, zc_:zc_ + 512], 1.0 / E, None,
                        ALU.mult)
        v.tensor_tensor(t2, m, m, ALU.mult)
        v.tensor_tensor(var, t1, t2, ALU.subtract)
        v.tensor_scalar(var, var, EPS, None, ALU.add)
        v.tensor_scalar(t1, var, RC3, RC2, ALU.mult, ALU.add)
        v.tensor_tensor(t2, t1, var, ALU.mult)
        v.tensor_scalar(t1, t2, RC1, None, ALU.add)
        v.tensor_tensor(t2, t1, var, ALU.mult)
        v.tensor_scalar(y, t2, RC0, None, ALU.add)
        v.tensor_scalar(y, y, 0.02, None, ALU.max)
        for _ in range(4):
            v.tensor_tensor(t1, y, y, ALU.mult)
            v.tensor_tensor(t2, t1, var, ALU.mult)
            v.tensor_scalar(t1, t2, -0.5, 1.5, ALU.mult, ALU.add)
            v.tensor_tensor(y, y, t1, ALU.mult)
        v.tensor_copy(rows3b[0:1, 0:512], y)
        v.tensor_tensor(rows3b[0:1, 512:1024], y,
                        rows3b[0:1, 0:512], ALU.subtract)
        return v.tensor_copy(rows3b[0:1, 1024:1536], m)

    def collect_post(it):
        if it == NT - 1:
            zstate["wide"] = True
        itp = 0
        ctw = list(cs_ready[it])
        if it >= 1:
            ctw.append(("P3", proj_done[it - 1]))
        def ldc(s, it=it):
            src = ctx_s[:, it * 512:(it + 1) * 512].rearrange(
                "(c p) i -> p c i", p=128)
            return s.dma_start(ctx_t[:, :], src)
        lp.op("sp", ldc, waits=W(*ctw), inc="LC", amt=16, tag="ldc")
        l3c = sch.v("LC")

        zrec = []
        for fc in range(NE):
            gidx = it * NE + fc
            xw = []
            if gidx >= 2:
                xw.append(("V3", zglob[gidx - 2]))
            def ldx3(s, it=it, fc=fc):
                return s.dma_start(
                    x_ring[:, (fc % 2) * 512:(fc % 2) * 512 + 512],
                    xT[fc * 128:(fc + 1) * 128,
                       it * 512:(it + 1) * 512])
            lp.op("sp", ldx3, waits=W(*xw), inc=f"LX{fc % 2}", amt=16,
                  tag="ldxr")
            x_rec_fc = (f"LX{fc % 2}", 16 * (gidx // 2 + 1))

            zw_, zc_ = z7use()
            pw = W(("LWO", LW_WOUT), ("LC", l3c)) + zw_
            def pmm(t, fc=fc, zc_=zc_):
                last = None
                for ec in range(NE):
                    last = t.matmul(
                        P[:, zc_:zc_ + 512],
                        wout_sb[:, ec * 1024 + fc * 128:
                                ec * 1024 + fc * 128 + 128],
                        ctx_t[:, ec * 512:ec * 512 + 512],
                        start=(ec == 0), stop=(ec == NE - 1))
                return last
            lp.op("pe", pmm, waits=pw, inc="P3", tag="proj")
            p3v = sch.v("P3")
            def zop(v, fc=fc, zc_=zc_):
                return v.scalar_tensor_tensor(
                    z[:, fc * 512:(fc + 1) * 512],
                    P[:, zc_:zc_ + 512],
                    bo[:, fc:fc + 1],
                    x_ring[:, (fc % 2) * 512:(fc % 2) * 512 + 512],
                    ALU.add, ALU.add)
            zv = lp.op("dve", zop, waits=W(("P3", p3v), x_rec_fc),
                       inc="V3", tag="z")
            z7done((sems["V3"], zv))
            zrec.append(zv)
            zglob.append(zv)
        proj_done[it] = sch.v("P3")

        def ln_phase(zt, zwait, store_out):
            for fc in range(NE):
                def sqop(v, fc=fc, zt=zt):
                    return v.tensor_tensor(
                        sq8[:, fc * 512:(fc + 1) * 512],
                        zt[:, fc * 512:(fc + 1) * 512],
                        zt[:, fc * 512:(fc + 1) * 512], ALU.mult)
                lp.op("dve", sqop, waits=W(("V3", zwait[fc])), inc="V3",
                      tag="sq")
            sqv = sch.v("V3")

            zw_, zc_ = z7use()
            def s1g(t, zt=zt, zc_=zc_):
                last = None
                for fc in range(NE):
                    last = t.matmul(P[0:1, zc_:zc_ + 512], ones_f32[:, :],
                                    zt[:, fc * 512:(fc + 1) * 512],
                                    start=(fc == 0), stop=(fc == NE - 1),
                                    skip_group_check=True)
                return last
            lp.op("pe", s1g, waits=W(("V3", zwait[-1])) + zw_,
                  inc="P3", tag="s1")
            s1v = sch.v("P3")
            def lra(v, zc_=zc_):
                return v.tensor_scalar(rows3[0:1, 0:512],
                                       P[0:1, zc_:zc_ + 512],
                                       1.0 / E, None, ALU.mult)
            mv = lp.op("dve", lra, waits=W(("P3", s1v)), inc="V3",
                       tag="rowsa")
            z7done((sems["V3"], mv))

            zw_, zc_ = z7use()
            def s2g(t, zc_=zc_):
                last = None
                for fc in range(NE):
                    last = t.matmul(P[0:1, zc_:zc_ + 512], ones_bf[:, 0:1],
                                    sq8[:, fc * 512:(fc + 1) * 512],
                                    start=(fc == 0), stop=(fc == NE - 1),
                                    skip_group_check=True)
                return last
            lp.op("pe", s2g, waits=W(("V3", sqv)) + zw_, inc="P3",
                  tag="s2")
            s2v = sch.v("P3")
            rv = lp.op("dve",
                       (lambda v, zc_=zc_: ln_rows_b(v, zc_)),
                       waits=W(("P3", s2v)), inc="V3", tag="rowsb")
            z7done((sems["V3"], rv))

            zw_, zc_ = z7use()
            def mbg(t, zc_=zc_):
                return t.matmul(P[:, zc_:zc_ + 512], ones_bf[0:1, :],
                                rows3b[0:1, 1024:1536], start=True,
                                stop=True, tile_position=(0, 0),
                                skip_group_check=True)
            lp.op("pe", mbg, waits=W(("V3", rv)) + zw_, inc="P3",
                  tag="mb")
            mbv = sch.v("P3")
            def mbe(v, zc_=zc_):
                return v.tensor_copy(mb_sb[:, :], P[:, zc_:zc_ + 512])
            mbev = lp.op("dve", mbe, waits=W(("P3", mbv)), inc="V3",
                         tag="mbe")
            z7done((sems["V3"], mbev))

            zw_, zc_ = z7use()
            def rsg(t, zc_=zc_):
                t.matmul(P[:, zc_:zc_ + 512], ones_bf[0:1, :],
                         rows3b[0:1, 0:512], start=True, stop=False,
                         tile_position=(0, 0), skip_group_check=True)
                return t.matmul(P[:, zc_:zc_ + 512], ones_bf[0:1, :],
                                rows3b[0:1, 512:1024], start=False,
                                stop=True, tile_position=(0, 0),
                                skip_group_check=True)
            lp.op("pe", rsg, waits=W(("V3", rv)) + zw_, inc="P3",
                  tag="rs")
            rsv = sch.v("P3")
            def rse(v, zc_=zc_):
                return v.tensor_copy(rb3_sb[:, :], P[:, zc_:zc_ + 512])
            rsev = lp.op("dve", rse, waits=W(("P3", rsv)), inc="V3",
                         tag="rse")
            z7done((sems["V3"], rsev))
            bevv = rsev

            outrec = []
            for fc in range(NE):
                if not store_out:
                    def nm(v, fc=fc, zt=zt):
                        v.tensor_tensor(sq[:, :],
                                        zt[:, fc * 512:(fc + 1) * 512],
                                        mb_sb[:, :], ALU.subtract)
                        v.tensor_tensor(y1f[:, fc * 512:(fc + 1) * 512],
                                        sq[:, :], rb3_sb[:, :], ALU.mult)
                        return v.tensor_copy(
                            y1b[:, fc * 512:(fc + 1) * 512],
                            y1f[:, fc * 512:(fc + 1) * 512])
                    lp.op("dve", nm, waits=W(("V3", bevv)), inc="V3",
                          tag="y1")
                    outrec.append(sch.v("V3"))
                else:
                    out_n = ln_state["out_n"]
                    ow = [("V3", bevv)]
                    if out_n >= 2:
                        ow.append((f"OS{out_n % 2}", 16 * (out_n // 2)))
                    def nm(v, fc=fc, zt=zt, out_n=out_n):
                        v.tensor_tensor(sq[:, :],
                                        zt[:, fc * 512:(fc + 1) * 512],
                                        mb_sb[:, :], ALU.subtract)
                        return v.tensor_tensor(
                            out_t[:, (out_n % 2) * 512:
                                  (out_n % 2) * 512 + 512],
                            sq[:, :], rb3_sb[:, :], ALU.mult)
                    ov = lp.op("dve", nm, waits=W(*ow), inc="V3",
                               tag="out")
                    def sto(s, fc=fc, it=it, out_n=out_n):
                        return s.dma_start(
                            yT[fc * 128:(fc + 1) * 128,
                               it * 512:(it + 1) * 512],
                            out_t[:, (out_n % 2) * 512:
                                  (out_n % 2) * 512 + 512])
                    lp.op("sp", sto, waits=W(("V3", ov)),
                          inc=f"OS{out_n % 2}", amt=16, tag="sto")
                    ln_state["out_n"] += 1
                    outrec.append(ov)
            return outrec

        y1rec = ln_phase(z, zrec, False)
        y1_all = y1rec[-1]

        p3now = sch.v("P3")
        for sl, wt in ((0, w_ffn), (1, w_a1)):
            for ec in range(NE):
                def ldw3(gq, sl=sl, wt=wt, ec=ec):
                    return gq.dma_start(
                        wslot[sl][:, ec * 1024:(ec + 1) * 1024],
                        wt[ec * 128:(ec + 1) * 128, :])
                lp.op("gp", ldw3, waits=W(("P3", p3now)),
                      inc=f"LW{sl}", amt=16, tag="ldwf")
        lw_w = ("LW0", sch.v("LW0"))
        lw_a1 = ("LW1", sch.v("LW1"))

        def ffn_stage(rhs, wsl, lww, inwait, evac_dst, tagp):
            recs = []
            for dc in range(NE):
                zw_, zc_ = z7use()
                hw = W(lww, ("V3", inwait)) + zw_
                def hmm(t, dc=dc, rhs=rhs, wsl=wsl, zc_=zc_):
                    last = None
                    for ec in range(NE):
                        last = t.matmul(
                            P[:, zc_:zc_ + 512],
                            wslot[wsl][:, ec * 1024 + dc * 128:
                                       ec * 1024 + dc * 128 + 128],
                            rhs[:, ec * 512:ec * 512 + 512],
                            start=(ec == 0), stop=(ec == NE - 1))
                    return last
                lp.op("pe", hmm, waits=hw, inc="P3", tag=tagp)
                pv3 = sch.v("P3")
                def hev(v, dc=dc, evac_dst=evac_dst, zc_=zc_):
                    return v.tensor_copy(
                        evac_dst[:, dc * 512:(dc + 1) * 512],
                        P[:, zc_:zc_ + 512])
                hv = lp.op("dve", hev, waits=W(("P3", pv3)), inc="V3",
                           tag=tagp + "e")
                z7done((sems["V3"], hv))
                recs.append(hv)
            return recs

        h1rec = ffn_stage(y1b, 0, lw_w, y1_all, h1b, "h1")
        h1_last_p3 = sch.v("P3")
        for ec in range(NE):
            def lda2(gq, ec=ec):
                return gq.dma_start(wslot[0][:, ec * 1024:(ec + 1) * 1024],
                                    w_a2[ec * 128:(ec + 1) * 128, :])
            lp.op("gp", lda2, waits=W(("P3", h1_last_p3)), inc="LW0",
                  amt=16, tag="lda")
        lw_a2 = ("LW0", sch.v("LW0"))
        h2rec = ffn_stage(h1b, 1, lw_a1, h1rec[-1], h2b, "h2")

        z2rec = []
        for dc in range(NE):
            zw_, zc_ = z7use()
            hw = W(lw_a2, ("V3", h2rec[-1])) + zw_
            def h3mm(t, dc=dc, zc_=zc_):
                last = None
                for ec in range(NE):
                    last = t.matmul(
                        P[:, zc_:zc_ + 512],
                        wslot[0][:, ec * 1024 + dc * 128:
                                 ec * 1024 + dc * 128 + 128],
                        h2b[:, ec * 512:ec * 512 + 512],
                        start=(ec == 0), stop=(ec == NE - 1))
                return last
            lp.op("pe", h3mm, waits=hw, inc="P3", tag="h3")
            pv3 = sch.v("P3")
            aw = [("P3", pv3)]
            if dc >= 2:
                aw.append(("V3", z2rec[dc - 2]))
            def th(a, dc=dc, zc_=zc_):
                return a.activation(
                    ffn_t[:, (dc % 2) * 512:(dc % 2) * 512 + 512],
                    P[:, zc_:zc_ + 512],
                    AF.Tanh, bias=bf[:, dc:dc + 1], scale=1.0)
            a3v = lp.op("act", th, waits=W(*aw), inc="A3", tag="tanh")
            z7done((sems["A3"], a3v))
            def z2op(v, dc=dc):
                return v.tensor_tensor(
                    z[:, dc * 512:(dc + 1) * 512],
                    y1f[:, dc * 512:(dc + 1) * 512],
                    ffn_t[:, (dc % 2) * 512:(dc % 2) * 512 + 512],
                    ALU.add)
            z2rec.append(lp.op("dve", z2op, waits=W(("A3", a3v)),
                               inc="V3", tag="z2"))

        ln_phase(z, z2rec, True)

    g = 0
    u = 0
    mk_count = 0
    last_st_w = {}

    for it in range(NT):
        for p in range(NP):
            lw = list(ST1_ALL_W)
            if u >= 2:
                lw.append(("SC", 32 * (u - 1)))
            def ldk(s, p=p, u=u):
                return s.dma_start(
                    k_pair[:, (u % 2) * S:(u % 2) * S + S],
                    k_s[p * 128:(p + 1) * 128, :])
            la.op("sp", ldk, waits=W(*lw), inc=f"KVK{u % 2}", amt=16,
                  tag="ldk")
            kvk_rdy = (f"KVK{u % 2}", 16 * (u // 2 + 1))
            vw = list(ST1_ALL_W)
            if u >= 1:
                vw.append(("CX", 4 * u))
            for h in range(2):
                def ldv(s, p=p, h=h):
                    dst = v_pair[:, :].rearrange(
                        "p (j c) -> p j c",
                        c=130)[:, :, 65 * h:65 * h + 64]
                    src = v_s[p, :, :].rearrange(
                        "(c p) d -> p c d",
                        p=128)[:, :, 64 * h:64 * h + 64]
                    return s.dma_start(dst, src)
                la.op("sp", ldv, waits=W(*vw), inc="KVV", amt=16,
                      tag="ldv")
            kvv_rdy = ("KVV", 32 * (u + 1))
            qw = list(ST1_ALL_W)
            if u >= 2:
                qw.append(("SC", 32 * (u - 1)))
            def ldq(s, p=p, it=it, u=u):
                return s.dma_start(
                    q_buf[:, (u % 2) * 512:(u % 2) * 512 + 512],
                    q_s[p * 128:(p + 1) * 128, it * 512:(it + 1) * 512])
            la.op("sp", ldq, waits=W(*qw), inc=f"KVQ{u % 2}", amt=16,
                  tag="ldq")
            kvq_rdy = (f"KVQ{u % 2}", 16 * (u // 2 + 1))

            def ctx_half(b, half, u=u, kvv_rdy=kvv_rdy):
                gb_loc = u * NB + b
                cwaits = W(("EX", 2 * (u * 32 + b * BST + BST)),
                           ("ACC", gb_loc), ("E1", E1_ALL), kvv_rdy)
                pc = 2048 + half * 512
                vo = 65 * half
                def cgr(t, b=b, u=u, pc=pc, vo=vo, half=half):
                    last = None
                    for sub in range(BST):
                        jc = b * BST + sub
                        eb = ((u * 32 + jc) % 14) * 1024 + 512 * half
                        last = t.matmul(
                            P[0:65, pc:pc + 512],
                            v_pair[:, jc * 130 + vo:jc * 130 + vo + 65],
                            E_sb[:, eb:eb + 512],
                            start=(sub == 0), stop=(sub == BST - 1),
                            skip_group_check=True)
                    return last
                la.op("pe", cgr, waits=cwaits,
                      inc=("CX" if half == 1 else None), tag="cx")
                if half == 0:
                    return
                cxv = sch.v("CX")
                def accop(v, b=b, u=u):
                    slot = (u % 2) * 1024
                    if b == 0:
                        return v.tensor_copy(
                            acc[0:65, slot:slot + 1024],
                            P[0:65, 2048:3072])
                    return v.tensor_tensor(
                        acc[0:65, slot:slot + 1024], P[0:65, 2048:3072],
                        acc[0:65, slot:slot + 1024], ALU.add)
                la.op("dve", accop, waits=W(("CX", cxv)), inc="ACC",
                      tag="acc")

            for b in range(NB):
                for sub in range(BST):
                    jc = b * BST + sub
                    cA = 512 * SBANKS[(2 * g) % 5]
                    cB = 512 * SBANKS[(2 * g + 1) % 5]
                    swaits = [kvk_rdy, kvq_rdy, ("E1", E1_ALL)]
                    if g >= 2:
                        swaits.append(("EX", 2 * g - 3))
                    def smm(t, jc=jc, cA=cA, cB=cB, u=u):
                        t.matmul(
                            P[:, cA:cA + 512],
                            k_pair[0:64, (u % 2) * S + jc * 128:
                                   (u % 2) * S + jc * 128 + 128],
                            q_buf[0:64,
                                  (u % 2) * 512:(u % 2) * 512 + 512],
                            start=True, stop=True, tile_position=(0, 0))
                        return t.matmul(
                            P[:, cB:cB + 512],
                            k_pair[64:128, (u % 2) * S + jc * 128:
                                   (u % 2) * S + jc * 128 + 128],
                            q_buf[64:128,
                                  (u % 2) * 512:(u % 2) * 512 + 512],
                            start=True, stop=True, tile_position=(64, 0))
                    la.op("pe", smm, waits=W(*swaits), inc="SC",
                          tag="sc")
                    scv = sch.v("SC")

                    for (dst, src, wd) in mask_windows(it, jc):
                        def madd(v, dst=dst, src=src, wd=wd, cA=cA,
                                 cB=cB):
                            v.tensor_tensor(
                                P[:, cA + dst:cA + dst + wd],
                                P[:, cA + dst:cA + dst + wd],
                                maskT[:, src:src + wd], ALU.add)
                            return v.tensor_tensor(
                                P[:, cB + dst:cB + dst + wd],
                                P[:, cB + dst:cB + dst + wd],
                                maskT[:, src:src + wd], ALU.add)
                        la.op("dve", madd,
                              waits=W(("SC", scv), ("LDB", LDB_ALL)),
                              inc="MK", tag="mk")
                        mk_count += 1

                    ewaits = [("SC", scv), ("MK", mk_count)]
                    if g >= 14:
                        ewaits.append(("CX", (g - 14) // 8 + 1))
                    def expA(a, g=g, cA=cA):
                        return a.activation(
                            E_sb[:, (g % 14) * 1024:
                                 (g % 14) * 1024 + 512],
                            P[:, cA:cA + 512],
                            AF.Exp, bias=cexp[:, :], scale=0.125)
                    la.op("act", expA, waits=W(*ewaits), inc="EX",
                          tag="exp")
                    def expB(a, g=g, cB=cB):
                        return a.activation(
                            E_sb[:, (g % 14) * 1024 + 512:
                                 (g % 14) * 1024 + 1024],
                            P[:, cB:cB + 512],
                            AF.Exp, bias=cexp[:, :], scale=0.125)
                    la.op("act", expB, waits=W(), inc="EX", tag="exp")
                    g += 1
                    if b >= 1 and sub == 2:
                        ctx_half(b - 1, 0)
                    if b >= 1 and sub == 5:
                        ctx_half(b - 1, 1)
            ctx_half(NB - 1, 0)
            ctx_half(NB - 1, 1)

            accv = sch.v("ACC")
            def st1_(v, u=u):
                slot = (u % 2) * 1024
                return v.reciprocal(rows3[64:65, 0:1024],
                                    acc[64:65, slot:slot + 1024])
            n1av = lnm.op("dve", st1_, W(("ACC", accv)), inc="N1A",
                          tag="nq_st1")
            def st1d(sq_, u=u):
                return sq_.dma_start(den_d[u % 2:u % 2 + 1, :],
                                     rows3[64:65, 0:1024])
            dnw = [("N1A", n1av)]
            if u >= 1:
                dnw.append(("BC", 16 * u))
            n1v = lnm.op("sp", st1d, W(*dnw), inc="N1", amt=16,
                         tag="dnst")
            def st1b(sq_, u=u):
                dst2, src2 = broadcast_tensor_aps(
                    den_bc[0:64, :], den_d[u % 2:u % 2 + 1, :])
                return sq_.dma_start(dst2, src2)
            bcw = [("N1", n1v)]
            if u >= 1:
                bcw.append(("N2", 2 * u))
            bcv = lnm.op("sp", st1b, W(*bcw), inc="BC", amt=16,
                         tag="dnbc")

            def st3a(v, u=u):
                slot = (u % 2) * 1024
                return v.tensor_tensor(Rb_sb[0:64, 0:512],
                                       acc[0:64, slot:slot + 512],
                                       den_bc[0:64, 0:512], ALU.mult)
            s3w = [("BC", bcv)]
            if u >= 1:
                s3w.append(("N3", u))
            lnm.op("dve", st3a, W(*s3w), inc="N2", tag="nq_st3")

            def st3b(v, u=u):
                slot = (u % 2) * 1024
                return v.tensor_tensor(
                    Rb_sb[0:64, 512:1024],
                    acc[0:64, slot + 512:slot + 1024],
                    den_bc[0:64, 512:1024], ALU.mult)
            lnm.op("dve", st3b, W(), inc="N2", tag="nq_st3")

            st4w = []
            if u >= 1 and (u - 1) in last_st_w:
                st4w = last_st_w[u - 1]
            def st4_(v, p=p, u=u):
                v.tensor_scalar(stage_sb[0:64, 0:512],
                                Rb_sb[0:64, 0:512], bv[0:64, p:p + 1],
                                None, ALU.add)
                return v.tensor_scalar(
                    stage_sb[0:64, 512:1024],
                    Rb_sb[0:64, 512:1024],
                    bvB[0:64, p:p + 1], None, ALU.add)
            n3v = lnm.op("dve", st4_, W(("N2", 2 * u + 2), *st4w),
                         inc="N3", tag="nq_st4")

            for half in range(2):
                def stc(s, p=p, it=it, half=half):
                    so = half * 512
                    return s.dma_start(
                        ctx_s[p * 128 + half * 64:
                              p * 128 + half * 64 + 64,
                              it * 512:(it + 1) * 512],
                        stage_sb[0:64, so:so + 512])
                ls.op("sp", stc, W(("N3", n3v)), inc=f"CS{u % 2}",
                      amt=16, tag="stc")
            last_st_w[u] = [(f"CS{u % 2}", sch.v(f"CS{u % 2}"))]
            u += 1

        cs_ready[it] = [("CS0", sch.v("CS0")), ("CS1", sch.v("CS1"))]
        if 3 in phases:
            collect_post(it)

    if 3 not in phases:
        CS_ALL_W = [("CS0", sch.v("CS0")), ("CS1", sch.v("CS1"))]
        def dbg3(gq):
            return gq.dma_start(yT[:, :], ctx_s[:, :])
        la.op("gp", dbg3, waits=W(*CS_ALL_W), inc="OS0", amt=16)
        la.op("sp", lambda s_: s_.nop(), waits=W(("OS0", sch.v("OS0"))))
        sch.run_merge()
        return _finish(nc, sch, sems, st)

    la.op("sp", lambda s: s.nop(),
          waits=W(("OS0", sch.v("OS0")), ("OS1", sch.v("OS1"))),
          tag="fin")
    sch.run_merge()
    return _finish(nc, sch, sems, st)


def _finish(nc, sch, sems, st):
    global LAST_SCHED
    LAST_SCHED = sch

    def section(eng_obj, eng_name):
        waited = {}
        for waits, fn, inc, amt, tag in sch.ops[eng_name]:
            for (sem, val) in waits:
                key = id(sem)
                if waited.get(key, -1) < val:
                    eng_obj.wait_ge(sem, val)
                    waited[key] = val
            inst = fn(eng_obj)
            if inc is not None:
                inst.then_inc(sems[inc], amt)

    with nc.Block() as block:
        @block.tensor
        def _(t):
            section(t, "pe")

        @block.scalar
        def _(a):
            section(a, "act")

        @block.vector
        def _(v):
            section(v, "dve")

        @block.gpsimd
        def _(gq):
            section(gq, "gp")

        @block.sync
        def _(s):
            section(s, "sp")

    st.close()
    return nc


_NC_CACHE = None


def kernel(L0_attn_input_embeddings, L0_attn_token_ids,
           L0_attn_in_proj_weight, L0_attn_in_proj_bias,
           L0_attn_out_proj_weight, L0_attn_out_proj_bias,
           L0_ffn_weight, L0_ffn_aux_weight1, L0_ffn_aux_weight2,
           L0_ffn_bias):
    global _NC_CACHE
    _ = L0_attn_token_ids
    x = np.asarray(L0_attn_input_embeddings, dtype=np.float32)
    w_in = np.asarray(L0_attn_in_proj_weight, dtype=np.float32)
    b_in = np.asarray(L0_attn_in_proj_bias, dtype=np.float32)
    w_out = np.asarray(L0_attn_out_proj_weight, dtype=np.float32)
    b_out = np.asarray(L0_attn_out_proj_bias, dtype=np.float32)
    w_f = np.asarray(L0_ffn_weight, dtype=np.float32)
    a1 = np.asarray(L0_ffn_aux_weight1, dtype=np.float32)
    a2 = np.asarray(L0_ffn_aux_weight2, dtype=np.float32)
    b_f = np.asarray(L0_ffn_bias, dtype=np.float32)

    w_inT = np.ascontiguousarray(w_in.T)
    w_outT = np.ascontiguousarray(w_out.T)
    bq = np.ascontiguousarray(b_in[0:E].reshape(NE, 128).T)
    bk = np.ascontiguousarray(b_in[E:2 * E].reshape(NE, 128).T)
    bvv = np.ascontiguousarray(b_in[2 * E:3 * E].reshape(NE, 128).T)
    bvB = np.ascontiguousarray(np.roll(b_in[2 * E:3 * E].reshape(NE, 128).T,
                                       -64, axis=0))
    bo = np.ascontiguousarray(b_out.reshape(NE, 128).T)
    bff = np.ascontiguousarray(b_f.reshape(NE, 128).T)
    pp, cc = np.arange(128)[:, None], np.arange(144)[None, :]
    import ml_dtypes
    maskT = np.where((cc - pp >= 0) & (cc - pp <= WINDOW), NEG,
                     0.0).astype(ml_dtypes.bfloat16)

    if _NC_CACHE is None:
        _NC_CACHE = build_nc()
    nc = _NC_CACHE

    in_maps = []
    for c in range(8):
        in_maps.append({
            "xT": np.ascontiguousarray(x[c].T),
            "w_inT": w_inT, "w_outT": w_outT,
            "w_ffn": w_f, "w_a1": a1, "w_a2": a2,
            "bq": bq, "bk": bk, "bv": bvv, "bvB": bvB, "bo": bo,
            "bf": bff, "maskT": maskT,
        })
    res = run_bass_kernel_spmd(nc, in_maps, core_ids=list(range(8)))
    out = np.stack([np.ascontiguousarray(r["yT"].T) for r in res.results])
    return out.astype(np.float32)


# version 20
# speedup vs baseline: 1.0643x; 1.0643x over previous
import numpy as np
import concourse.bass as bass
import concourse.mybir as mybir
from concourse.bass import broadcast_tensor_aps
from concourse.bass_utils import run_bass_kernel_spmd

F32 = mybir.dt.float32
BF16 = mybir.dt.bfloat16
AF = mybir.ActivationFunctionType
ALU = mybir.AluOpType

B, S, E = 8, 4096, 1024
NP = 8
NT = S // 512
NJ = S // 128
NE = E // 128
WINDOW = 16
NEG = -1.0e9
CSHIFT = 3.0
EPS = 1e-5
RC0, RC1, RC2, RC3 = (1.9871303416114157, -0.9757659737364119,
                      0.19554620527220662, -0.012522044838806663)
BST = 8
NB = NJ // BST
SBANKS = (0, 1, 2, 3, 6)

ENGS = ("pe", "act", "dve", "gp", "sp")
LAST_SCHED = None


DUR = {
    "qk": 2.0, "v": 2.2, "qkev": 0.7, "vev": 0.6,
    "sc": 0.7, "mk": 0.35, "exp": 1.05, "cx": 1.9, "acc": 1.3,
    "nq_st1": 7.0, "nq_hl": 2.6, "bc": 0.5, "nq_st3": 1.3,
    "nq_st4": 0.8,
    "proj": 1.9, "z": 0.7, "sq": 0.65, "s1": 3.6, "s2": 1.9,
    "rowsa": 0.65, "rowsb": 13.0, "mb": 0.5, "rs": 0.6,
    "mbe": 0.65, "rse": 0.65, "y1": 1.7, "out": 1.3,
    "h1": 1.9, "h2": 1.9, "h3": 1.9, "h1e": 0.65, "h2e": 0.65,
    "tanh": 0.8, "z2": 0.65, "ms": 0.3, "vones": 0.3, "fin": 0.1,
    "warm": 0.06,
}
DMA_TAGS = {"ldxf": 2048, "ldb": 4, "ldw": 192, "qkst": 128, "vst": 128,
            "ldk": 1024, "ldv": 512, "ldq": 128, "stc": 64, "ldc": 512,
            "ldxr": 256, "sto": 256, "ldwo": 512, "ldwf": 2048,
            "dnst": 4, "dnbc": 256,
            "lda": 2048}


class Sched:

    def __init__(self):
        self.cnt = {}
        self.ops = {e: [] for e in ENGS}
        self.lanes = []
        self.semmap = {}

    def lane(self):
        ln = Lane(self)
        self.lanes.append(ln)
        return ln

    def v(self, sem):
        return self.cnt.get(sem, 0)

    def merge(self):
        heads = {e: [0] * len(self.lanes) for e in ENGS}
        etime = {e: 0.0 for e in ENGS}
        semt = {}

        def semtime(name, val):
            lst = semt.get(name)
            if lst is None or not lst or lst[-1][0] < val:
                return None
            from bisect import bisect_left
            vals = [x[0] for x in lst]
            i = bisect_left(vals, val)
            return lst[i][1]

        total = sum(len(ln.q[e]) for ln in self.lanes for e in ENGS)
        done = 0
        while done < total:
            progress = False
            best = None
            for e in ENGS:
                for li, ln in enumerate(self.lanes):
                    hi = heads[e][li]
                    if hi >= len(ln.q[e]):
                        continue
                    waits, fn, inc, amt, tag = ln.q[e][hi]
                    t = etime[e]
                    ok = True
                    for (h, vv) in waits:
                        st_ = semtime(self.semmap[id(h)], vv)
                        if st_ is None:
                            ok = False
                            break
                        t = max(t, st_)
                    if not ok:
                        continue
                    if best is None or t < best[0]:
                        best = (t, e, li)
            if best is None:
                raise RuntimeError("merge deadlock: no selectable op")
            t, e, li = best
            ln = self.lanes[li]
            waits, fn, inc, amt, tag = ln.q[e][heads[e][li]]
            heads[e][li] += 1
            done += 1
            progress = True
            base = tag.rstrip("0123456789")
            if base in DMA_TAGS:
                dur = 1.2 + DMA_TAGS[base] * 2 / 400.0
                issue = 0.25
                fin_t = max(t, etime[e]) + issue + dur
                etime[e] = max(t, etime[e]) + issue
            else:
                dur = DUR.get(base, DUR.get(tag, 0.6))
                fin_t = max(t, etime[e]) + 0.12 + dur
                etime[e] = fin_t
            self.ops[e].append((waits, fn, inc, amt, tag))
            if inc is not None:
                nv = self.cnt2.get(inc, 0) + amt
                self.cnt2[inc] = nv
                semt.setdefault(inc, []).append((nv, fin_t))
        self.est_span = max(etime.values())

    def run_merge(self):
        self.cnt2 = {}
        self.merge()


class Lane:
    def __init__(self, sch):
        self.sch = sch
        self.q = {e: [] for e in ENGS}

    def op(self, eng, fn, waits=(), inc=None, amt=1, tag=""):
        val = None
        if inc is not None:
            val = self.sch.cnt.get(inc, 0) + amt
            self.sch.cnt[inc] = val
        self.q[eng].append((tuple(w for w in waits if w[1] is not None
                                  and w[1] > 0), fn, inc, amt, tag))
        return val

    def v(self, sem):
        return self.sch.v(sem)


def mask_windows(it, jc):
    sp = 128 * jc - 512 * it
    if sp == -128:
        return [(0, 128, 16)]
    if 0 <= sp <= 384:
        return [(sp, 0, min(144, 512 - sp))]
    return []


def build_nc(phases=(1, 2, 3)):
    nc = bass.Bass("TRN2", target_bir_lowering=False)

    xT = nc.dram_tensor("xT", [E, S], F32, kind="ExternalInput")
    w_inT = nc.dram_tensor("w_inT", [E, 3 * E], F32, kind="ExternalInput")
    w_outT = nc.dram_tensor("w_outT", [E, E], F32, kind="ExternalInput")
    w_ffn = nc.dram_tensor("w_ffn", [E, E], F32, kind="ExternalInput")
    w_a1 = nc.dram_tensor("w_a1", [E, E], F32, kind="ExternalInput")
    w_a2 = nc.dram_tensor("w_a2", [E, E], F32, kind="ExternalInput")
    bq_d = nc.dram_tensor("bq", [128, NE], F32, kind="ExternalInput")
    bk_d = nc.dram_tensor("bk", [128, NE], F32, kind="ExternalInput")
    bv_d = nc.dram_tensor("bv", [128, NE], F32, kind="ExternalInput")
    bvB_d = nc.dram_tensor("bvB", [128, NE], F32, kind="ExternalInput")
    bo_d = nc.dram_tensor("bo", [128, NE], F32, kind="ExternalInput")
    bf_d = nc.dram_tensor("bf", [128, NE], F32, kind="ExternalInput")
    mk_d = nc.dram_tensor("maskT", [128, 144], BF16, kind="ExternalInput")
    yT = nc.dram_tensor("yT", [E, S], F32, kind="ExternalOutput")
    q_s = nc.dram_tensor("q_s", [E, S], BF16, kind="Internal")
    den_d = nc.dram_tensor("den_d", [2, 1024], F32, kind="Internal")
    k_s = nc.dram_tensor("k_s", [E, S], BF16, kind="Internal")
    v_s = nc.dram_tensor("v_s", [NP, S, 128], BF16, kind="Internal")
    ctx_s = nc.dram_tensor("ctx_s", [E, S], BF16, kind="Internal")

    from contextlib import ExitStack
    st = ExitStack()
    sb = lambda n, sh, dt: st.enter_context(nc.sbuf_tensor(n, sh, dt))

    big = sb("big", [128, 49152], BF16)
    x_fm = big[:, 0:32768]
    wp = [big[:, 32768 + i * 3072:32768 + (i + 1) * 3072] for i in (0, 1)]
    qk_buf = [big[:, 38912 + i * 512:38912 + (i + 1) * 512] for i in range(4)]
    v_buf = [big[:, 40960 + i * 512:40960 + (i + 1) * 512] for i in (0, 1)]
    wout_sb = big[:, 0:8192]
    wslot = [big[:, 8192 + i * 8192:8192 + (i + 1) * 8192] for i in (0, 1)]
    z = big[:, 24576:32768].bitcast(F32)
    y1f = big[:, 32768:40960].bitcast(F32)
    y1b = big[:, 40960:45056]
    h2b = y1b
    h1b = big[:, 45056:49152]
    sq8 = h1b

    k_pair = sb("k_pair", [128, 2 * S], BF16)
    ctx_t = sb("ctx_t", [128, S], BF16)
    v_pair = sb("v_pair", [128, 32 * 130], BF16)
    q_buf = sb("q_buf", [128, 1024], BF16)
    E_sb = sb("E_sb", [128, 14336], BF16)
    acc = sb("acc", [128, 2048], F32)
    Rb_sb = sb("Rb_sb", [128, 1024], F32)
    den_bc = sb("den_bc", [64, 1024], F32)
    stage_sb = sb("stage_sb", [128, 1024], BF16)
    rows3 = sb("rows3", [128, 1536], F32)
    rows3b = sb("rows3b", [128, 1536], BF16)
    mb_sb = sb("mb_sb", [128, 512], F32)
    rb3_sb = sb("rb3_sb", [128, 512], F32)
    x_ring = sb("x_ring", [128, 1024], F32)
    sq = sb("sq", [128, 512], F32)
    ffn_t = sb("ffn_t", [128, 1024], F32)
    out_t = sb("out_t", [128, 1024], F32)
    bq = sb("bq_s", [128, NE], F32)
    bk = sb("bk_s", [128, NE], F32)
    bv = sb("bv_s", [128, NE], F32)
    bvB = sb("bvB_s", [128, NE], F32)
    bo = sb("bo_s", [128, NE], F32)
    bf = sb("bf_s", [128, NE], F32)
    maskT = sb("maskT_s", [128, 144], BF16)
    ones_bf = sb("ones_bf", [128, 128], BF16)
    ones_f32 = sb("ones_f32", [128, 1], F32)
    cexp = sb("cexp", [128, 1], F32)

    P = st.enter_context(nc.psum_tensor("P", [128, 4096], F32))
    Z6, Z7 = 3072, 3584

    sems = {}
    for name in ("LDX", "LDB", "LDW0", "LDW1", "P1", "E1",
                 "STQ0", "STQ1", "STQ2", "STQ3", "STV0", "STV1",
                 "KVK0", "KVK1", "KVV", "KVQ0", "KVQ1",
                 "SC", "MK", "EX", "CX", "ACC", "N1", "N1A", "BC", "N2", "N3",
                 "CS0", "CS1", "LC", "LX0", "LX1",
                 "LWO", "LW0", "LW1", "P3", "V3", "A3",
                 "OS0", "OS1"):
        sems[name] = st.enter_context(nc.semaphore(name))

    sch = Sched()
    sch.semmap = {id(h): n for n, h in sems.items()}
    la = sch.lane()
    lnm = sch.lane()
    lp = sch.lane()
    ls = sch.lane()
    zone7 = [None]

    def W(*pairs):
        return [(sems[s], v) for (s, v) in pairs
                if v is not None and v > 0]

    for ec in range(NE):
        la.op("gp", lambda g, ec=ec: g.dma_start(
            x_fm[:, ec * 4096:(ec + 1) * 4096],
            xT[ec * 128:(ec + 1) * 128, :]), inc="LDX", amt=16, tag="ldxf")
    for dd, t in ((bq_d, bq), (bk_d, bk), (bv_d, bv), (bvB_d, bvB),
                  (bo_d, bo), (bf_d, bf), (mk_d, maskT)):
        la.op("sp", lambda s, dd=dd, t=t: s.dma_start(t[:, :], dd[:, :]),
               inc="LDB", amt=16, tag="ldb")
    la.op("dve", lambda v: v.memset(ones_bf[:, :], 1.0), inc="E1", tag="ms")
    la.op("dve", lambda v: v.memset(ones_f32[:, :], 1.0), inc="E1",
          tag="ms")
    la.op("dve", lambda v: v.memset(cexp[:, :], -CSHIFT), inc="E1",
          tag="ms")

    def vones(v):
        vv = v_pair[:, :].rearrange("p (j c) -> p j c", c=130)
        v.memset(vv[:, :, 64:65], 1.0)
        return v.memset(vv[:, :, 129:130], 1.0)
    la.op("dve", vones, inc="E1", tag="vones")
    LDX_ALL = sch.v("LDX")
    LDB_ALL = sch.v("LDB")
    MS0 = sch.v("E1")

    wp_done = [None] * NP
    pair_p1_end = [None] * NP
    qk_ev, v_ev = [], []
    qk_n = 0
    v_n = 0
    for p in range(NP):
        gate = []
        if p >= 2:
            gate.append(("P1", pair_p1_end[p - 2]))
        for kind in range(3):
            def ldwp(g, p=p, kind=kind):
                out = wp[p % 2][:, :].rearrange(
                    "p (ec k d) -> p ec k d", ec=NE, k=3,
                    d=128)[:, :, kind, :]
                src = w_inT[:, kind * E + p * 128:
                            kind * E + (p + 1) * 128].rearrange(
                    "(ec p) f -> p ec f", p=128)
                return g.dma_start(out, src)
            la.op("gp", ldwp, waits=W(*gate), inc=f"LDW{p % 2}",
                  amt=16, tag="ldw")
        wp_done[p] = 48 * (p // 2 + 1)

        for kind in range(2):
            for tt in range(NT):
                waits = [("LDX", LDX_ALL), (f"LDW{p % 2}", wp_done[p])]
                if qk_n >= 2:
                    waits.append(("E1", qk_ev[qk_n - 2]))
                bank = qk_n % 2
                def mmg(t, p=p, kind=kind, tt=tt, bank=bank):
                    last = None
                    for ec in range(NE):
                        last = t.matmul(
                            P[:, bank * 512:bank * 512 + 512],
                            wp[p % 2][:, ec * 384 + kind * 128:
                                      ec * 384 + kind * 128 + 128],
                            x_fm[:, ec * 4096 + tt * 512:
                                 ec * 4096 + tt * 512 + 512],
                            start=(ec == 0), stop=(ec == NE - 1))
                    return last
                la.op("pe", mmg, waits=W(*waits), inc="P1", tag="qk")
                pv = sch.v("P1")
                ew = [("P1", pv), ("E1", MS0), ("LDB", LDB_ALL)]
                if qk_n >= 4:
                    ew.append((f"STQ{qk_n % 4}", 16 * (qk_n // 4)))
                bias = bq if kind == 0 else bk
                buf = qk_n % 4
                def ev(v, p=p, bank=bank, bias=bias, buf=buf):
                    return v.tensor_scalar(
                        qk_buf[buf][:, :],
                        P[:, bank * 512:bank * 512 + 512],
                        bias[:, p:p + 1], None, ALU.add)
                la.op("dve", ev, waits=W(*ew), inc="E1", tag="qkev")
                qk_ev.append(sch.v("E1"))
                dst = q_s if kind == 0 else k_s
                def stq(s, p=p, tt=tt, buf=buf, dst=dst):
                    return s.dma_start(
                        dst[p * 128:(p + 1) * 128,
                            tt * 512:(tt + 1) * 512],
                        qk_buf[buf][:, :])
                la.op("sp", stq, waits=W(("E1", qk_ev[-1])),
                      inc=f"STQ{qk_n % 4}", amt=16, tag="qkst")
                qk_n += 1

        for jq in range(NJ // 4):
            waits = [("LDX", LDX_ALL), (f"LDW{p % 2}", wp_done[p])]
            if v_n >= 2:
                waits.append(("E1", v_ev[v_n - 2]))
            bank = v_n % 2
            def vmm(t, p=p, jq=jq, bank=bank):
                last = None
                pbase = 1024 + bank * 512
                for sub in range(4):
                    jc = jq * 4 + sub
                    for ec in range(NE):
                        last = t.matmul(
                            P[:, pbase + sub * 128:pbase + sub * 128 + 128],
                            x_fm[:, ec * 4096 + jc * 128:
                                 ec * 4096 + jc * 128 + 128],
                            wp[p % 2][:, ec * 384 + 256:ec * 384 + 384],
                            start=(ec == 0), stop=(ec == NE - 1))
                return last
            la.op("pe", vmm, waits=W(*waits), inc="P1", tag="v")
            pv = sch.v("P1")
            ew = [("P1", pv)]
            if v_n >= 2:
                ew.append((f"STV{v_n % 2}", 16 * (v_n // 2)))
            def vev(v, bank=bank):
                return v.tensor_copy(v_buf[bank][:, :],
                                     P[:, 1024 + bank * 512:
                                       1024 + bank * 512 + 512])
            la.op("dve", vev, waits=W(*ew), inc="E1", tag="vev")
            v_ev.append(sch.v("E1"))
            def stv(s, p=p, jq=jq, bank=bank):
                out = v_s[p, jq * 512:(jq + 1) * 512, :].rearrange(
                    "(c p) d -> p c d", p=128)
                return s.dma_start(out, v_buf[bank][:, :])
            la.op("sp", stv, waits=W(("E1", v_ev[-1])),
                  inc=f"STV{v_n % 2}", amt=16, tag="vst")
            v_n += 1
        pair_p1_end[p] = sch.v("P1")

    ST1_ALL_W = [("STQ0", sch.v("STQ0")), ("STQ1", sch.v("STQ1")),
                 ("STQ2", sch.v("STQ2")), ("STQ3", sch.v("STQ3")),
                 ("STV0", sch.v("STV0")), ("STV1", sch.v("STV1"))]
    E1_ALL = sch.v("E1")
    P1_ALL = sch.v("P1")

    if 2 not in phases:
        def dbg(gq):
            return gq.dma_start(yT[:, 0:2048],
                                q_s[:, :].rearrange("p (a b) -> p a b",
                                                    a=2)[:, 0, :])
        la.op("gp", dbg, waits=W(*ST1_ALL_W), inc="OS0", amt=16)
        def dbg2(gq):
            return gq.dma_start(yT[:, 2048:4096],
                                k_s[:, :].rearrange("p (a b) -> p a b",
                                                    a=2)[:, 0, :])
        la.op("gp", dbg2, waits=W(*ST1_ALL_W), inc="OS0", amt=16)
        la.op("sp", lambda s_: s_.nop(), waits=W(("OS0", sch.v("OS0"))))
        sch.run_merge()
        return _finish(nc, sch, sems, st)

    for ec in range(NE):
        def ldwo(gq, ec=ec):
            return gq.dma_start(wout_sb[:, ec * 1024:(ec + 1) * 1024],
                                w_outT[ec * 128:(ec + 1) * 128, :])
        la.op("gp", ldwo, waits=W(("P1", P1_ALL)), inc="LWO", amt=16,
              tag="ldwo")
    LW_WOUT = sch.v("LWO")

    ln_state = {"out_n": 0}
    zglob = []
    cs_ready = [None] * NT
    proj_done = [None] * NT

    zstate = {"wide": False, "last": [None, None], "rr": 0,
              "first": True}

    def z7use():
        if not zstate["wide"]:
            w = zone7[0]
            return [w] if w is not None else [], Z7
        b = zstate["rr"]
        zstate["rr"] ^= 1
        zstate["cur"] = b
        w = zstate["last"][b]
        ws = [w] if w is not None else []
        if zstate["first"]:
            ws = ws + W(("EX", 4096))
            if b == 1:
                zstate["first"] = False
        return ws, b * 512

    def z7done(marker):
        if not zstate["wide"]:
            zone7[0] = marker
        else:
            zstate["last"][zstate["cur"]] = marker

    def ln_rows_b(v, zc_):
        m = rows3[0:1, 0:512]
        var = rows3[0:1, 512:1024]
        y = rows3[0:1, 1024:1536]
        t1 = mb_sb[0:1, :]
        t2 = rb3_sb[0:1, :]
        v.tensor_scalar(t1, P[0:1, zc_:zc_ + 512], 1.0 / E, None,
                        ALU.mult)
        v.tensor_tensor(t2, m, m, ALU.mult)
        v.tensor_tensor(var, t1, t2, ALU.subtract)
        v.tensor_scalar(var, var, EPS, None, ALU.add)
        v.tensor_scalar(t1, var, RC3, RC2, ALU.mult, ALU.add)
        v.tensor_tensor(t2, t1, var, ALU.mult)
        v.tensor_scalar(t1, t2, RC1, None, ALU.add)
        v.tensor_tensor(t2, t1, var, ALU.mult)
        v.tensor_scalar(y, t2, RC0, None, ALU.add)
        v.tensor_scalar(y, y, 0.02, None, ALU.max)
        for _ in range(4):
            v.tensor_tensor(t1, y, y, ALU.mult)
            v.tensor_tensor(t2, t1, var, ALU.mult)
            v.tensor_scalar(t1, t2, -0.5, 1.5, ALU.mult, ALU.add)
            v.tensor_tensor(y, y, t1, ALU.mult)
        v.tensor_copy(rows3b[0:1, 0:512], y)
        v.tensor_tensor(rows3b[0:1, 512:1024], y,
                        rows3b[0:1, 0:512], ALU.subtract)
        return v.tensor_copy(rows3b[0:1, 1024:1536], m)

    def collect_post(it):
        if it == NT - 1:
            zstate["wide"] = True
        itp = 0
        ctw = list(cs_ready[it])
        if it >= 1:
            ctw.append(("P3", proj_done[it - 1]))
        def ldc(s, it=it):
            src = ctx_s[:, it * 512:(it + 1) * 512].rearrange(
                "(c p) i -> p c i", p=128)
            return s.dma_start(ctx_t[:, :], src)
        lp.op("sp", ldc, waits=W(*ctw), inc="LC", amt=16, tag="ldc")
        l3c = sch.v("LC")

        zrec = []
        for fc in range(NE):
            gidx = it * NE + fc
            xw = []
            if gidx >= 2:
                xw.append(("V3", zglob[gidx - 2]))
            def ldx3(s, it=it, fc=fc):
                return s.dma_start(
                    x_ring[:, (fc % 2) * 512:(fc % 2) * 512 + 512],
                    xT[fc * 128:(fc + 1) * 128,
                       it * 512:(it + 1) * 512])
            lp.op("sp", ldx3, waits=W(*xw), inc=f"LX{fc % 2}", amt=16,
                  tag="ldxr")
            x_rec_fc = (f"LX{fc % 2}", 16 * (gidx // 2 + 1))

            zw_, zc_ = z7use()
            pw = W(("LWO", LW_WOUT), ("LC", l3c)) + zw_
            def pmm(t, fc=fc, zc_=zc_):
                last = None
                for ec in range(NE):
                    last = t.matmul(
                        P[:, zc_:zc_ + 512],
                        wout_sb[:, ec * 1024 + fc * 128:
                                ec * 1024 + fc * 128 + 128],
                        ctx_t[:, ec * 512:ec * 512 + 512],
                        start=(ec == 0), stop=(ec == NE - 1))
                return last
            lp.op("pe", pmm, waits=pw, inc="P3", tag="proj")
            p3v = sch.v("P3")
            def zop(v, fc=fc, zc_=zc_):
                return v.scalar_tensor_tensor(
                    z[:, fc * 512:(fc + 1) * 512],
                    P[:, zc_:zc_ + 512],
                    bo[:, fc:fc + 1],
                    x_ring[:, (fc % 2) * 512:(fc % 2) * 512 + 512],
                    ALU.add, ALU.add)
            zv = lp.op("dve", zop, waits=W(("P3", p3v), x_rec_fc),
                       inc="V3", tag="z")
            z7done((sems["V3"], zv))
            zrec.append(zv)
            zglob.append(zv)
        proj_done[it] = sch.v("P3")

        def ln_phase(zt, zwait, store_out):
            for fc in range(NE):
                def sqop(v, fc=fc, zt=zt):
                    return v.tensor_tensor(
                        sq8[:, fc * 512:(fc + 1) * 512],
                        zt[:, fc * 512:(fc + 1) * 512],
                        zt[:, fc * 512:(fc + 1) * 512], ALU.mult)
                lp.op("dve", sqop, waits=W(("V3", zwait[fc])), inc="V3",
                      tag="sq")
            sqv = sch.v("V3")

            zw_, zc_ = z7use()
            def s1g(t, zt=zt, zc_=zc_):
                last = None
                for fc in range(NE):
                    last = t.matmul(P[0:1, zc_:zc_ + 512], ones_f32[:, :],
                                    zt[:, fc * 512:(fc + 1) * 512],
                                    start=(fc == 0), stop=(fc == NE - 1),
                                    skip_group_check=True)
                return last
            lp.op("pe", s1g, waits=W(("V3", zwait[-1])) + zw_,
                  inc="P3", tag="s1")
            s1v = sch.v("P3")
            def lra(v, zc_=zc_):
                return v.tensor_scalar(rows3[0:1, 0:512],
                                       P[0:1, zc_:zc_ + 512],
                                       1.0 / E, None, ALU.mult)
            mv = lp.op("dve", lra, waits=W(("P3", s1v)), inc="V3",
                       tag="rowsa")
            z7done((sems["V3"], mv))

            zw_, zc_ = z7use()
            def s2g(t, zc_=zc_):
                last = None
                for fc in range(NE):
                    last = t.matmul(P[0:1, zc_:zc_ + 512], ones_bf[:, 0:1],
                                    sq8[:, fc * 512:(fc + 1) * 512],
                                    start=(fc == 0), stop=(fc == NE - 1),
                                    skip_group_check=True)
                return last
            lp.op("pe", s2g, waits=W(("V3", sqv)) + zw_, inc="P3",
                  tag="s2")
            s2v = sch.v("P3")
            rv = lp.op("dve",
                       (lambda v, zc_=zc_: ln_rows_b(v, zc_)),
                       waits=W(("P3", s2v)), inc="V3", tag="rowsb")
            z7done((sems["V3"], rv))

            zw_, zc_ = z7use()
            def mbg(t, zc_=zc_):
                return t.matmul(P[:, zc_:zc_ + 512], ones_bf[0:1, :],
                                rows3b[0:1, 1024:1536], start=True,
                                stop=True, tile_position=(0, 0),
                                skip_group_check=True)
            lp.op("pe", mbg, waits=W(("V3", rv)) + zw_, inc="P3",
                  tag="mb")
            mbv = sch.v("P3")
            def mbe(v, zc_=zc_):
                return v.tensor_copy(mb_sb[:, :], P[:, zc_:zc_ + 512])
            mbev = lp.op("dve", mbe, waits=W(("P3", mbv)), inc="V3",
                         tag="mbe")
            z7done((sems["V3"], mbev))

            zw_, zc_ = z7use()
            def rsg(t, zc_=zc_):
                t.matmul(P[:, zc_:zc_ + 512], ones_bf[0:1, :],
                         rows3b[0:1, 0:512], start=True, stop=False,
                         tile_position=(0, 0), skip_group_check=True)
                return t.matmul(P[:, zc_:zc_ + 512], ones_bf[0:1, :],
                                rows3b[0:1, 512:1024], start=False,
                                stop=True, tile_position=(0, 0),
                                skip_group_check=True)
            lp.op("pe", rsg, waits=W(("V3", rv)) + zw_, inc="P3",
                  tag="rs")
            rsv = sch.v("P3")
            def rse(v, zc_=zc_):
                return v.tensor_copy(rb3_sb[:, :], P[:, zc_:zc_ + 512])
            rsev = lp.op("dve", rse, waits=W(("P3", rsv)), inc="V3",
                         tag="rse")
            z7done((sems["V3"], rsev))
            bevv = rsev

            outrec = []
            for fc in range(NE):
                if not store_out:
                    def nm(v, fc=fc, zt=zt):
                        v.tensor_tensor(sq[:, :],
                                        zt[:, fc * 512:(fc + 1) * 512],
                                        mb_sb[:, :], ALU.subtract)
                        v.tensor_tensor(y1f[:, fc * 512:(fc + 1) * 512],
                                        sq[:, :], rb3_sb[:, :], ALU.mult)
                        return v.tensor_copy(
                            y1b[:, fc * 512:(fc + 1) * 512],
                            y1f[:, fc * 512:(fc + 1) * 512])
                    lp.op("dve", nm, waits=W(("V3", bevv)), inc="V3",
                          tag="y1")
                    outrec.append(sch.v("V3"))
                else:
                    out_n = ln_state["out_n"]
                    ow = [("V3", bevv)]
                    if out_n >= 2:
                        ow.append((f"OS{out_n % 2}", 16 * (out_n // 2)))
                    def nm(v, fc=fc, zt=zt, out_n=out_n):
                        v.tensor_tensor(sq[:, :],
                                        zt[:, fc * 512:(fc + 1) * 512],
                                        mb_sb[:, :], ALU.subtract)
                        return v.tensor_tensor(
                            out_t[:, (out_n % 2) * 512:
                                  (out_n % 2) * 512 + 512],
                            sq[:, :], rb3_sb[:, :], ALU.mult)
                    ov = lp.op("dve", nm, waits=W(*ow), inc="V3",
                               tag="out")
                    def sto(s, fc=fc, it=it, out_n=out_n):
                        return s.dma_start(
                            yT[fc * 128:(fc + 1) * 128,
                               it * 512:(it + 1) * 512],
                            out_t[:, (out_n % 2) * 512:
                                  (out_n % 2) * 512 + 512])
                    lp.op("sp", sto, waits=W(("V3", ov)),
                          inc=f"OS{out_n % 2}", amt=16, tag="sto")
                    ln_state["out_n"] += 1
                    outrec.append(ov)
            return outrec

        y1rec = ln_phase(z, zrec, False)
        y1_all = y1rec[-1]

        p3now = sch.v("P3")
        for sl, wt in ((0, w_ffn), (1, w_a1)):
            for ec in range(NE):
                def ldw3(gq, sl=sl, wt=wt, ec=ec):
                    return gq.dma_start(
                        wslot[sl][:, ec * 1024:(ec + 1) * 1024],
                        wt[ec * 128:(ec + 1) * 128, :])
                lp.op("gp", ldw3, waits=W(("P3", p3now)),
                      inc=f"LW{sl}", amt=16, tag="ldwf")
        lw_w = ("LW0", sch.v("LW0"))
        lw_a1 = ("LW1", sch.v("LW1"))

        def ffn_stage(rhs, wsl, lww, inwait, evac_dst, tagp):
            recs = []
            for dc in range(NE):
                zw_, zc_ = z7use()
                hw = W(lww, ("V3", inwait)) + zw_
                def hmm(t, dc=dc, rhs=rhs, wsl=wsl, zc_=zc_):
                    last = None
                    for ec in range(NE):
                        last = t.matmul(
                            P[:, zc_:zc_ + 512],
                            wslot[wsl][:, ec * 1024 + dc * 128:
                                       ec * 1024 + dc * 128 + 128],
                            rhs[:, ec * 512:ec * 512 + 512],
                            start=(ec == 0), stop=(ec == NE - 1))
                    return last
                lp.op("pe", hmm, waits=hw, inc="P3", tag=tagp)
                pv3 = sch.v("P3")
                def hev(v, dc=dc, evac_dst=evac_dst, zc_=zc_):
                    return v.tensor_copy(
                        evac_dst[:, dc * 512:(dc + 1) * 512],
                        P[:, zc_:zc_ + 512])
                hv = lp.op("dve", hev, waits=W(("P3", pv3)), inc="V3",
                           tag=tagp + "e")
                z7done((sems["V3"], hv))
                recs.append(hv)
            return recs

        h1rec = ffn_stage(y1b, 0, lw_w, y1_all, h1b, "h1")
        h1_last_p3 = sch.v("P3")
        for ec in range(NE):
            def lda2(gq, ec=ec):
                return gq.dma_start(wslot[0][:, ec * 1024:(ec + 1) * 1024],
                                    w_a2[ec * 128:(ec + 1) * 128, :])
            lp.op("gp", lda2, waits=W(("P3", h1_last_p3)), inc="LW0",
                  amt=16, tag="lda")
        lw_a2 = ("LW0", sch.v("LW0"))
        h2rec = ffn_stage(h1b, 1, lw_a1, h1rec[-1], h2b, "h2")

        z2rec = []
        for dc in range(NE):
            zw_, zc_ = z7use()
            hw = W(lw_a2, ("V3", h2rec[-1])) + zw_
            def h3mm(t, dc=dc, zc_=zc_):
                last = None
                for ec in range(NE):
                    last = t.matmul(
                        P[:, zc_:zc_ + 512],
                        wslot[0][:, ec * 1024 + dc * 128:
                                 ec * 1024 + dc * 128 + 128],
                        h2b[:, ec * 512:ec * 512 + 512],
                        start=(ec == 0), stop=(ec == NE - 1))
                return last
            lp.op("pe", h3mm, waits=hw, inc="P3", tag="h3")
            pv3 = sch.v("P3")
            aw = [("P3", pv3)]
            if dc >= 2:
                aw.append(("V3", z2rec[dc - 2]))
            def th(a, dc=dc, zc_=zc_):
                return a.activation(
                    ffn_t[:, (dc % 2) * 512:(dc % 2) * 512 + 512],
                    P[:, zc_:zc_ + 512],
                    AF.Tanh, bias=bf[:, dc:dc + 1], scale=1.0)
            a3v = lp.op("act", th, waits=W(*aw), inc="A3", tag="tanh")
            z7done((sems["A3"], a3v))
            def z2op(v, dc=dc):
                return v.tensor_tensor(
                    z[:, dc * 512:(dc + 1) * 512],
                    y1f[:, dc * 512:(dc + 1) * 512],
                    ffn_t[:, (dc % 2) * 512:(dc % 2) * 512 + 512],
                    ALU.add)
            z2rec.append(lp.op("dve", z2op, waits=W(("A3", a3v)),
                               inc="V3", tag="z2"))

        ln_phase(z, z2rec, True)

    g = 0
    u = 0
    mk_count = 0
    last_st_w = {}

    for it in range(NT):
        for p in range(NP):
            lw = list(ST1_ALL_W)
            if u >= 2:
                lw.append(("SC", 32 * (u - 1)))
            def ldk(s, p=p, u=u):
                return s.dma_start(
                    k_pair[:, (u % 2) * S:(u % 2) * S + S],
                    k_s[p * 128:(p + 1) * 128, :])
            la.op("sp", ldk, waits=W(*lw), inc=f"KVK{u % 2}", amt=16,
                  tag="ldk")
            kvk_rdy = (f"KVK{u % 2}", 16 * (u // 2 + 1))
            vw = list(ST1_ALL_W)
            if u >= 1:
                vw.append(("CX", 4 * u))
            for h in range(2):
                def ldv(s, p=p, h=h):
                    dst = v_pair[:, :].rearrange(
                        "p (j c) -> p j c",
                        c=130)[:, :, 65 * h:65 * h + 64]
                    src = v_s[p, :, :].rearrange(
                        "(c p) d -> p c d",
                        p=128)[:, :, 64 * h:64 * h + 64]
                    return s.dma_start(dst, src)
                la.op("sp", ldv, waits=W(*vw), inc="KVV", amt=16,
                      tag="ldv")
            kvv_rdy = ("KVV", 32 * (u + 1))
            qw = list(ST1_ALL_W)
            if u >= 2:
                qw.append(("SC", 32 * (u - 1)))
            def ldq(s, p=p, it=it, u=u):
                return s.dma_start(
                    q_buf[:, (u % 2) * 512:(u % 2) * 512 + 512],
                    q_s[p * 128:(p + 1) * 128, it * 512:(it + 1) * 512])
            la.op("sp", ldq, waits=W(*qw), inc=f"KVQ{u % 2}", amt=16,
                  tag="ldq")
            kvq_rdy = (f"KVQ{u % 2}", 16 * (u // 2 + 1))

            def ctx_half(b, half, u=u, kvv_rdy=kvv_rdy):
                gb_loc = u * NB + b
                cwaits = W(("EX", 2 * (u * 32 + b * BST + BST)),
                           ("ACC", gb_loc), ("E1", E1_ALL), kvv_rdy)
                pc = 2048 + half * 512
                vo = 65 * half
                def cgr(t, b=b, u=u, pc=pc, vo=vo, half=half):
                    last = None
                    for sub in range(BST):
                        jc = b * BST + sub
                        eb = ((u * 32 + jc) % 14) * 1024 + 512 * half
                        last = t.matmul(
                            P[0:65, pc:pc + 512],
                            v_pair[:, jc * 130 + vo:jc * 130 + vo + 65],
                            E_sb[:, eb:eb + 512],
                            start=(sub == 0), stop=(sub == BST - 1),
                            skip_group_check=True)
                    return last
                la.op("pe", cgr, waits=cwaits,
                      inc=("CX" if half == 1 else None), tag="cx")
                if half == 0:
                    return
                cxv = sch.v("CX")
                def accop(v, b=b, u=u):
                    slot = (u % 2) * 1024
                    if b == 0:
                        return v.tensor_copy(
                            acc[0:65, slot:slot + 1024],
                            P[0:65, 2048:3072])
                    return v.tensor_tensor(
                        acc[0:65, slot:slot + 1024], P[0:65, 2048:3072],
                        acc[0:65, slot:slot + 1024], ALU.add)
                la.op("dve", accop, waits=W(("CX", cxv)), inc="ACC",
                      tag="acc")

            for b in range(NB):
                for sub in range(BST):
                    jc = b * BST + sub
                    cA = 512 * SBANKS[(2 * g) % 5]
                    cB = 512 * SBANKS[(2 * g + 1) % 5]
                    swaits = [kvk_rdy, kvq_rdy, ("E1", E1_ALL)]
                    if g >= 2:
                        swaits.append(("EX", 2 * g - 3))
                    def smm(t, jc=jc, cA=cA, cB=cB, u=u):
                        t.matmul(
                            P[:, cA:cA + 512],
                            k_pair[0:64, (u % 2) * S + jc * 128:
                                   (u % 2) * S + jc * 128 + 128],
                            q_buf[0:64,
                                  (u % 2) * 512:(u % 2) * 512 + 512],
                            start=True, stop=True, tile_position=(0, 0))
                        return t.matmul(
                            P[:, cB:cB + 512],
                            k_pair[64:128, (u % 2) * S + jc * 128:
                                   (u % 2) * S + jc * 128 + 128],
                            q_buf[64:128,
                                  (u % 2) * 512:(u % 2) * 512 + 512],
                            start=True, stop=True, tile_position=(64, 0))
                    la.op("pe", smm, waits=W(*swaits), inc="SC",
                          tag="sc")
                    scv = sch.v("SC")

                    for (dst, src, wd) in mask_windows(it, jc):
                        def madd(v, dst=dst, src=src, wd=wd, cA=cA,
                                 cB=cB):
                            v.tensor_tensor(
                                P[:, cA + dst:cA + dst + wd],
                                P[:, cA + dst:cA + dst + wd],
                                maskT[:, src:src + wd], ALU.add)
                            return v.tensor_tensor(
                                P[:, cB + dst:cB + dst + wd],
                                P[:, cB + dst:cB + dst + wd],
                                maskT[:, src:src + wd], ALU.add)
                        la.op("dve", madd,
                              waits=W(("SC", scv), ("LDB", LDB_ALL)),
                              inc="MK", tag="mk")
                        mk_count += 1

                    ewaits = [("SC", scv), ("MK", mk_count)]
                    if g >= 14:
                        ewaits.append(("CX", (g - 14) // 8 + 1))
                    if cB == cA + 512:
                        def expF(a, g=g, cA=cA):
                            return a.activation(
                                E_sb[:, (g % 14) * 1024:
                                     (g % 14) * 1024 + 1024],
                                P[:, cA:cA + 1024],
                                AF.Exp, bias=cexp[:, :], scale=0.125)
                        la.op("act", expF, waits=W(*ewaits), inc="EX",
                              amt=2, tag="expf")
                    else:
                        def expA(a, g=g, cA=cA):
                            return a.activation(
                                E_sb[:, (g % 14) * 1024:
                                     (g % 14) * 1024 + 512],
                                P[:, cA:cA + 512],
                                AF.Exp, bias=cexp[:, :], scale=0.125)
                        la.op("act", expA, waits=W(*ewaits), inc="EX",
                              tag="exp")
                        def expB(a, g=g, cB=cB):
                            return a.activation(
                                E_sb[:, (g % 14) * 1024 + 512:
                                     (g % 14) * 1024 + 1024],
                                P[:, cB:cB + 512],
                                AF.Exp, bias=cexp[:, :], scale=0.125)
                        la.op("act", expB, waits=W(), inc="EX",
                              tag="exp")
                    g += 1
                    if b >= 1 and sub == 2:
                        ctx_half(b - 1, 0)
                    if b >= 1 and sub == 5:
                        ctx_half(b - 1, 1)
            ctx_half(NB - 1, 0)
            ctx_half(NB - 1, 1)

            accv = sch.v("ACC")
            def st1_(v, u=u):
                slot = (u % 2) * 1024
                return v.reciprocal(rows3[64:65, 0:1024],
                                    acc[64:65, slot:slot + 1024])
            n1av = lnm.op("dve", st1_, W(("ACC", accv)), inc="N1A",
                          tag="nq_st1")
            def st1d(sq_, u=u):
                return sq_.dma_start(den_d[u % 2:u % 2 + 1, :],
                                     rows3[64:65, 0:1024])
            dnw = [("N1A", n1av)]
            if u >= 1:
                dnw.append(("BC", 16 * u))
            n1v = lnm.op("sp", st1d, W(*dnw), inc="N1", amt=16,
                         tag="dnst")
            def st1b(sq_, u=u):
                dst2, src2 = broadcast_tensor_aps(
                    den_bc[0:64, :], den_d[u % 2:u % 2 + 1, :])
                return sq_.dma_start(dst2, src2)
            bcw = [("N1", n1v)]
            if u >= 1:
                bcw.append(("N2", 2 * u))
            bcv = lnm.op("sp", st1b, W(*bcw), inc="BC", amt=16,
                         tag="dnbc")

            def st3a(v, u=u):
                slot = (u % 2) * 1024
                return v.tensor_tensor(Rb_sb[0:64, 0:512],
                                       acc[0:64, slot:slot + 512],
                                       den_bc[0:64, 0:512], ALU.mult)
            s3w = [("BC", bcv)]
            if u >= 1:
                s3w.append(("N3", u))
            lnm.op("dve", st3a, W(*s3w), inc="N2", tag="nq_st3")

            def st3b(v, u=u):
                slot = (u % 2) * 1024
                return v.tensor_tensor(
                    Rb_sb[0:64, 512:1024],
                    acc[0:64, slot + 512:slot + 1024],
                    den_bc[0:64, 512:1024], ALU.mult)
            lnm.op("dve", st3b, W(), inc="N2", tag="nq_st3")

            st4w = []
            if u >= 1 and (u - 1) in last_st_w:
                st4w = last_st_w[u - 1]
            def st4_(v, p=p, u=u):
                v.tensor_scalar(stage_sb[0:64, 0:512],
                                Rb_sb[0:64, 0:512], bv[0:64, p:p + 1],
                                None, ALU.add)
                return v.tensor_scalar(
                    stage_sb[0:64, 512:1024],
                    Rb_sb[0:64, 512:1024],
                    bvB[0:64, p:p + 1], None, ALU.add)
            n3v = lnm.op("dve", st4_, W(("N2", 2 * u + 2), *st4w),
                         inc="N3", tag="nq_st4")

            for half in range(2):
                def stc(s, p=p, it=it, half=half):
                    so = half * 512
                    return s.dma_start(
                        ctx_s[p * 128 + half * 64:
                              p * 128 + half * 64 + 64,
                              it * 512:(it + 1) * 512],
                        stage_sb[0:64, so:so + 512])
                ls.op("sp", stc, W(("N3", n3v)), inc=f"CS{u % 2}",
                      amt=16, tag="stc")
            last_st_w[u] = [(f"CS{u % 2}", sch.v(f"CS{u % 2}"))]
            u += 1

        cs_ready[it] = [("CS0", sch.v("CS0")), ("CS1", sch.v("CS1"))]
        if 3 in phases:
            collect_post(it)

    if 3 not in phases:
        CS_ALL_W = [("CS0", sch.v("CS0")), ("CS1", sch.v("CS1"))]
        def dbg3(gq):
            return gq.dma_start(yT[:, :], ctx_s[:, :])
        la.op("gp", dbg3, waits=W(*CS_ALL_W), inc="OS0", amt=16)
        la.op("sp", lambda s_: s_.nop(), waits=W(("OS0", sch.v("OS0"))))
        sch.run_merge()
        return _finish(nc, sch, sems, st)

    la.op("sp", lambda s: s.nop(),
          waits=W(("OS0", sch.v("OS0")), ("OS1", sch.v("OS1"))),
          tag="fin")
    sch.run_merge()
    return _finish(nc, sch, sems, st)


def _finish(nc, sch, sems, st):
    global LAST_SCHED
    LAST_SCHED = sch

    def section(eng_obj, eng_name):
        waited = {}
        for waits, fn, inc, amt, tag in sch.ops[eng_name]:
            for (sem, val) in waits:
                key = id(sem)
                if waited.get(key, -1) < val:
                    eng_obj.wait_ge(sem, val)
                    waited[key] = val
            inst = fn(eng_obj)
            if inc is not None:
                inst.then_inc(sems[inc], amt)

    with nc.Block() as block:
        @block.tensor
        def _(t):
            section(t, "pe")

        @block.scalar
        def _(a):
            section(a, "act")

        @block.vector
        def _(v):
            section(v, "dve")

        @block.gpsimd
        def _(gq):
            section(gq, "gp")

        @block.sync
        def _(s):
            section(s, "sp")

    st.close()
    return nc


_NC_CACHE = None


def kernel(L0_attn_input_embeddings, L0_attn_token_ids,
           L0_attn_in_proj_weight, L0_attn_in_proj_bias,
           L0_attn_out_proj_weight, L0_attn_out_proj_bias,
           L0_ffn_weight, L0_ffn_aux_weight1, L0_ffn_aux_weight2,
           L0_ffn_bias):
    global _NC_CACHE
    _ = L0_attn_token_ids
    x = np.asarray(L0_attn_input_embeddings, dtype=np.float32)
    w_in = np.asarray(L0_attn_in_proj_weight, dtype=np.float32)
    b_in = np.asarray(L0_attn_in_proj_bias, dtype=np.float32)
    w_out = np.asarray(L0_attn_out_proj_weight, dtype=np.float32)
    b_out = np.asarray(L0_attn_out_proj_bias, dtype=np.float32)
    w_f = np.asarray(L0_ffn_weight, dtype=np.float32)
    a1 = np.asarray(L0_ffn_aux_weight1, dtype=np.float32)
    a2 = np.asarray(L0_ffn_aux_weight2, dtype=np.float32)
    b_f = np.asarray(L0_ffn_bias, dtype=np.float32)

    w_inT = np.ascontiguousarray(w_in.T)
    w_outT = np.ascontiguousarray(w_out.T)
    bq = np.ascontiguousarray(b_in[0:E].reshape(NE, 128).T)
    bk = np.ascontiguousarray(b_in[E:2 * E].reshape(NE, 128).T)
    bvv = np.ascontiguousarray(b_in[2 * E:3 * E].reshape(NE, 128).T)
    bvB = np.ascontiguousarray(np.roll(b_in[2 * E:3 * E].reshape(NE, 128).T,
                                       -64, axis=0))
    bo = np.ascontiguousarray(b_out.reshape(NE, 128).T)
    bff = np.ascontiguousarray(b_f.reshape(NE, 128).T)
    pp, cc = np.arange(128)[:, None], np.arange(144)[None, :]
    import ml_dtypes
    maskT = np.where((cc - pp >= 0) & (cc - pp <= WINDOW), NEG,
                     0.0).astype(ml_dtypes.bfloat16)

    if _NC_CACHE is None:
        _NC_CACHE = build_nc()
    nc = _NC_CACHE

    in_maps = []
    for c in range(8):
        in_maps.append({
            "xT": np.ascontiguousarray(x[c].T),
            "w_inT": w_inT, "w_outT": w_outT,
            "w_ffn": w_f, "w_a1": a1, "w_a2": a2,
            "bq": bq, "bk": bk, "bv": bvv, "bvB": bvB, "bo": bo,
            "bf": bff, "maskT": maskT,
        })
    res = run_bass_kernel_spmd(nc, in_maps, core_ids=list(range(8)))
    out = np.stack([np.ascontiguousarray(r["yT"].T) for r in res.results])
    return out.astype(np.float32)


# version 21 (final)
# speedup vs baseline: 1.0878x; 1.0221x over previous
import numpy as np
import concourse.bass as bass
import concourse.mybir as mybir
from concourse.bass import broadcast_tensor_aps
from concourse.bass_utils import run_bass_kernel_spmd

F32 = mybir.dt.float32
BF16 = mybir.dt.bfloat16
AF = mybir.ActivationFunctionType
ALU = mybir.AluOpType

B, S, E = 8, 4096, 1024
NP = 8
NT = S // 512
NJ = S // 128
NE = E // 128
WINDOW = 16
NEG = -1.0e9
CSHIFT = 3.0
EPS = 1e-5
RC0, RC1, RC2, RC3 = (1.9871303416114157, -0.9757659737364119,
                      0.19554620527220662, -0.012522044838806663)
BST = 8
NB = NJ // BST
SBANKS = (0, 1, 2, 3, 6)

ENGS = ("pe", "act", "dve", "gp", "sp")
LAST_SCHED = None


DUR = {
    "qk": 2.0, "v": 2.2, "qkev": 0.7, "vev": 0.6,
    "sc": 0.7, "mk": 0.35, "exp": 1.05, "cx": 1.9, "acc": 1.3,
    "nq_st1": 7.0, "nq_hl": 2.6, "bc": 0.5, "nq_st3": 1.3,
    "nq_st4": 0.8,
    "proj": 1.9, "z": 0.7, "sq": 0.65, "s1": 3.6, "s2": 1.9,
    "rowsa": 0.65, "rowsb": 13.0, "mb": 0.5, "rs": 0.6,
    "mbe": 0.65, "rse": 0.65, "y1": 1.7, "out": 1.3,
    "h1": 1.9, "h2": 1.9, "h3": 1.9, "h1e": 0.65, "h2e": 0.65,
    "tanh": 0.8, "z2": 0.65, "ms": 0.3, "vones": 0.3, "fin": 0.1,
    "warm": 0.06,
}
DMA_TAGS = {"ldxf": 2048, "ldb": 4, "ldw": 192, "qkst": 128, "vst": 128,
            "ldk": 1024, "ldv": 512, "ldq": 128, "stc": 64, "ldc": 512,
            "ldxr": 256, "sto": 256, "ldwo": 512, "ldwf": 2048,
            "dnst": 4, "dnbc": 256,
            "lda": 2048}


class Sched:

    def __init__(self):
        self.cnt = {}
        self.ops = {e: [] for e in ENGS}
        self.lanes = []
        self.semmap = {}

    def lane(self):
        ln = Lane(self)
        self.lanes.append(ln)
        return ln

    def v(self, sem):
        return self.cnt.get(sem, 0)

    def merge(self):
        heads = {e: [0] * len(self.lanes) for e in ENGS}
        etime = {e: 0.0 for e in ENGS}
        semt = {}

        def semtime(name, val):
            lst = semt.get(name)
            if lst is None or not lst or lst[-1][0] < val:
                return None
            from bisect import bisect_left
            vals = [x[0] for x in lst]
            i = bisect_left(vals, val)
            return lst[i][1]

        total = sum(len(ln.q[e]) for ln in self.lanes for e in ENGS)
        done = 0
        while done < total:
            progress = False
            best = None
            for e in ENGS:
                for li, ln in enumerate(self.lanes):
                    hi = heads[e][li]
                    if hi >= len(ln.q[e]):
                        continue
                    waits, fn, inc, amt, tag = ln.q[e][hi]
                    t = etime[e]
                    ok = True
                    for (h, vv) in waits:
                        st_ = semtime(self.semmap[id(h)], vv)
                        if st_ is None:
                            ok = False
                            break
                        t = max(t, st_)
                    if not ok:
                        continue
                    if best is None or t < best[0]:
                        best = (t, e, li)
            if best is None:
                raise RuntimeError("merge deadlock: no selectable op")
            t, e, li = best
            ln = self.lanes[li]
            waits, fn, inc, amt, tag = ln.q[e][heads[e][li]]
            heads[e][li] += 1
            done += 1
            progress = True
            base = tag.rstrip("0123456789")
            if base in DMA_TAGS:
                dur = 1.2 + DMA_TAGS[base] * 2 / 400.0
                issue = 0.25
                fin_t = max(t, etime[e]) + issue + dur
                etime[e] = max(t, etime[e]) + issue
            else:
                dur = DUR.get(base, DUR.get(tag, 0.6))
                fin_t = max(t, etime[e]) + 0.12 + dur
                etime[e] = fin_t
            self.ops[e].append((waits, fn, inc, amt, tag))
            if inc is not None:
                nv = self.cnt2.get(inc, 0) + amt
                self.cnt2[inc] = nv
                semt.setdefault(inc, []).append((nv, fin_t))
        self.est_span = max(etime.values())

    def run_merge(self):
        self.cnt2 = {}
        self.merge()


class Lane:
    def __init__(self, sch):
        self.sch = sch
        self.q = {e: [] for e in ENGS}

    def op(self, eng, fn, waits=(), inc=None, amt=1, tag=""):
        val = None
        if inc is not None:
            val = self.sch.cnt.get(inc, 0) + amt
            self.sch.cnt[inc] = val
        self.q[eng].append((tuple(w for w in waits if w[1] is not None
                                  and w[1] > 0), fn, inc, amt, tag))
        return val

    def v(self, sem):
        return self.sch.v(sem)


def mask_windows(it, jc):
    sp = 128 * jc - 512 * it
    if sp == -128:
        return [(0, 128, 16)]
    if 0 <= sp <= 384:
        return [(sp, 0, min(144, 512 - sp))]
    return []


def build_nc(phases=(1, 2, 3)):
    nc = bass.Bass("TRN2", target_bir_lowering=False)

    xT = nc.dram_tensor("xT", [E, S], F32, kind="ExternalInput")
    w_inT = nc.dram_tensor("w_inT", [E, 3 * E], F32, kind="ExternalInput")
    w_outT = nc.dram_tensor("w_outT", [E, E], F32, kind="ExternalInput")
    w_ffn = nc.dram_tensor("w_ffn", [E, E], F32, kind="ExternalInput")
    w_a1 = nc.dram_tensor("w_a1", [E, E], F32, kind="ExternalInput")
    w_a2 = nc.dram_tensor("w_a2", [E, E], F32, kind="ExternalInput")
    bq_d = nc.dram_tensor("bq", [128, NE], F32, kind="ExternalInput")
    bk_d = nc.dram_tensor("bk", [128, NE], F32, kind="ExternalInput")
    bv_d = nc.dram_tensor("bv", [128, NE], F32, kind="ExternalInput")
    bvB_d = nc.dram_tensor("bvB", [128, NE], F32, kind="ExternalInput")
    bo_d = nc.dram_tensor("bo", [128, NE], F32, kind="ExternalInput")
    bf_d = nc.dram_tensor("bf", [128, NE], F32, kind="ExternalInput")
    mk_d = nc.dram_tensor("maskT", [128, 144], BF16, kind="ExternalInput")
    yT = nc.dram_tensor("yT", [E, S], F32, kind="ExternalOutput")
    q_s = nc.dram_tensor("q_s", [E, S], BF16, kind="Internal")
    den_d = nc.dram_tensor("den_d", [2, 1024], F32, kind="Internal")
    k_s = nc.dram_tensor("k_s", [E, S], BF16, kind="Internal")
    v_s = nc.dram_tensor("v_s", [NP, S, 128], BF16, kind="Internal")
    ctx_s = nc.dram_tensor("ctx_s", [E, S], BF16, kind="Internal")

    from contextlib import ExitStack
    st = ExitStack()
    sb = lambda n, sh, dt: st.enter_context(nc.sbuf_tensor(n, sh, dt))

    big = sb("big", [128, 49152], BF16)
    x_fm = big[:, 0:32768]
    wp = [big[:, 32768 + i * 3072:32768 + (i + 1) * 3072] for i in (0, 1)]
    qk_buf = [big[:, 38912 + i * 512:38912 + (i + 1) * 512] for i in range(4)]
    v_buf = [big[:, 40960 + i * 512:40960 + (i + 1) * 512] for i in (0, 1)]
    wout_sb = big[:, 0:8192]
    wslot = [big[:, 8192 + i * 8192:8192 + (i + 1) * 8192] for i in (0, 1)]
    z = big[:, 24576:32768].bitcast(F32)
    y1f = big[:, 32768:40960].bitcast(F32)
    y1b = big[:, 40960:45056]
    h2b = y1b
    h1b = big[:, 45056:49152]
    sq8 = h1b

    k_pair = sb("k_pair", [128, 2 * S], BF16)
    ctx_t = sb("ctx_t", [128, S], BF16)
    v_pair = sb("v_pair", [128, 32 * 130], BF16)
    q_buf = sb("q_buf", [128, 1024], BF16)
    E_sb = sb("E_sb", [128, 14336], BF16)
    acc = sb("acc", [128, 2048], F32)
    Rb_sb = sb("Rb_sb", [128, 1024], F32)
    den_bc = sb("den_bc", [64, 1024], F32)
    stage_sb = sb("stage_sb", [128, 1024], BF16)
    rows3 = sb("rows3", [128, 1536], F32)
    rows3b = sb("rows3b", [128, 1536], BF16)
    mb_sb = sb("mb_sb", [128, 512], F32)
    rb3_sb = sb("rb3_sb", [128, 512], F32)
    x_ring = sb("x_ring", [128, 1024], F32)
    sq = sb("sq", [128, 512], F32)
    ffn_t = sb("ffn_t", [128, 1024], F32)
    out_t = sb("out_t", [128, 1024], F32)
    bq = sb("bq_s", [128, NE], F32)
    bk = sb("bk_s", [128, NE], F32)
    bv = sb("bv_s", [128, NE], F32)
    bvB = sb("bvB_s", [128, NE], F32)
    bo = sb("bo_s", [128, NE], F32)
    bf = sb("bf_s", [128, NE], F32)
    maskT = sb("maskT_s", [128, 144], BF16)
    ones_bf = sb("ones_bf", [128, 128], BF16)
    ones_f32 = sb("ones_f32", [128, 1], F32)
    cexp = sb("cexp", [128, 1], F32)

    P = st.enter_context(nc.psum_tensor("P", [128, 4096], F32))
    Z6, Z7 = 3072, 3584

    sems = {}
    for name in ("LDX", "LDB", "LDW0", "LDW1", "P1", "E1",
                 "STQ0", "STQ1", "STQ2", "STQ3", "STV0", "STV1",
                 "KVK0", "KVK1", "KVV", "KVQ0", "KVQ1",
                 "SC", "MK", "EX", "CX", "ACC", "N1", "N1A", "BC", "N2", "N3",
                 "CS0", "CS1", "LC", "LX0", "LX1",
                 "LWO", "LW0", "LW1", "P3", "V3", "A3",
                 "OS0", "OS1"):
        sems[name] = st.enter_context(nc.semaphore(name))

    sch = Sched()
    sch.semmap = {id(h): n for n, h in sems.items()}
    la = sch.lane()
    lnm = sch.lane()
    lp = sch.lane()
    ls = sch.lane()
    zone7 = [None]

    def W(*pairs):
        return [(sems[s], v) for (s, v) in pairs
                if v is not None and v > 0]

    for ec in range(NE):
        la.op("gp", lambda g, ec=ec: g.dma_start(
            x_fm[:, ec * 4096:(ec + 1) * 4096],
            xT[ec * 128:(ec + 1) * 128, :]), inc="LDX", amt=16, tag="ldxf")
    for dd, t in ((bq_d, bq), (bk_d, bk), (bv_d, bv), (bvB_d, bvB),
                  (bo_d, bo), (bf_d, bf), (mk_d, maskT)):
        la.op("sp", lambda s, dd=dd, t=t: s.dma_start(t[:, :], dd[:, :]),
               inc="LDB", amt=16, tag="ldb")
    la.op("dve", lambda v: v.memset(ones_bf[:, :], 1.0), inc="E1", tag="ms")
    la.op("dve", lambda v: v.memset(ones_f32[:, :], 1.0), inc="E1",
          tag="ms")
    la.op("dve", lambda v: v.memset(cexp[:, :], -CSHIFT), inc="E1",
          tag="ms")

    def vones(v):
        vv = v_pair[:, :].rearrange("p (j c) -> p j c", c=130)
        v.memset(vv[:, :, 64:65], 1.0)
        return v.memset(vv[:, :, 129:130], 1.0)
    la.op("dve", vones, inc="E1", tag="vones")
    LDX_ALL = sch.v("LDX")
    LDB_ALL = sch.v("LDB")
    MS0 = sch.v("E1")

    wp_done = [None] * NP
    pair_p1_end = [None] * NP
    qk_ev, v_ev = [], []
    qk_n = 0
    v_n = 0
    for p in range(NP):
        gate = []
        if p >= 2:
            gate.append(("P1", pair_p1_end[p - 2]))
        for kind in range(3):
            def ldwp(g, p=p, kind=kind):
                out = wp[p % 2][:, :].rearrange(
                    "p (ec k d) -> p ec k d", ec=NE, k=3,
                    d=128)[:, :, kind, :]
                src = w_inT[:, kind * E + p * 128:
                            kind * E + (p + 1) * 128].rearrange(
                    "(ec p) f -> p ec f", p=128)
                return g.dma_start(out, src)
            la.op("gp", ldwp, waits=W(*gate), inc=f"LDW{p % 2}",
                  amt=16, tag="ldw")
        wp_done[p] = 48 * (p // 2 + 1)

        for kind in range(2):
            for tt in range(NT):
                waits = [("LDX", LDX_ALL), (f"LDW{p % 2}", wp_done[p])]
                if qk_n >= 2:
                    waits.append(("E1", qk_ev[qk_n - 2]))
                bank = qk_n % 2
                def mmg(t, p=p, kind=kind, tt=tt, bank=bank):
                    last = None
                    for ec in range(NE):
                        last = t.matmul(
                            P[:, bank * 512:bank * 512 + 512],
                            wp[p % 2][:, ec * 384 + kind * 128:
                                      ec * 384 + kind * 128 + 128],
                            x_fm[:, ec * 4096 + tt * 512:
                                 ec * 4096 + tt * 512 + 512],
                            start=(ec == 0), stop=(ec == NE - 1))
                    return last
                la.op("pe", mmg, waits=W(*waits), inc="P1", tag="qk")
                pv = sch.v("P1")
                ew = [("P1", pv), ("E1", MS0), ("LDB", LDB_ALL)]
                if qk_n >= 4:
                    ew.append((f"STQ{qk_n % 4}", 16 * (qk_n // 4)))
                bias = bq if kind == 0 else bk
                buf = qk_n % 4
                def ev(v, p=p, bank=bank, bias=bias, buf=buf):
                    return v.tensor_scalar(
                        qk_buf[buf][:, :],
                        P[:, bank * 512:bank * 512 + 512],
                        bias[:, p:p + 1], None, ALU.add)
                la.op("dve", ev, waits=W(*ew), inc="E1", tag="qkev")
                qk_ev.append(sch.v("E1"))
                dst = q_s if kind == 0 else k_s
                def stq(s, p=p, tt=tt, buf=buf, dst=dst):
                    return s.dma_start(
                        dst[p * 128:(p + 1) * 128,
                            tt * 512:(tt + 1) * 512],
                        qk_buf[buf][:, :])
                la.op("sp", stq, waits=W(("E1", qk_ev[-1])),
                      inc=f"STQ{qk_n % 4}", amt=16, tag="qkst")
                qk_n += 1

        for jq in range(NJ // 4):
            waits = [("LDX", LDX_ALL), (f"LDW{p % 2}", wp_done[p])]
            if v_n >= 2:
                waits.append(("E1", v_ev[v_n - 2]))
            bank = v_n % 2
            def vmm(t, p=p, jq=jq, bank=bank):
                last = None
                pbase = 1024 + bank * 512
                for sub in range(4):
                    jc = jq * 4 + sub
                    for ec in range(NE):
                        last = t.matmul(
                            P[:, pbase + sub * 128:pbase + sub * 128 + 128],
                            x_fm[:, ec * 4096 + jc * 128:
                                 ec * 4096 + jc * 128 + 128],
                            wp[p % 2][:, ec * 384 + 256:ec * 384 + 384],
                            start=(ec == 0), stop=(ec == NE - 1))
                return last
            la.op("pe", vmm, waits=W(*waits), inc="P1", tag="v")
            pv = sch.v("P1")
            ew = [("P1", pv)]
            if v_n >= 2:
                ew.append((f"STV{v_n % 2}", 16 * (v_n // 2)))
            def vev(v, bank=bank):
                return v.tensor_copy(v_buf[bank][:, :],
                                     P[:, 1024 + bank * 512:
                                       1024 + bank * 512 + 512])
            la.op("dve", vev, waits=W(*ew), inc="E1", tag="vev")
            v_ev.append(sch.v("E1"))
            def stv(s, p=p, jq=jq, bank=bank):
                out = v_s[p, jq * 512:(jq + 1) * 512, :].rearrange(
                    "(c p) d -> p c d", p=128)
                return s.dma_start(out, v_buf[bank][:, :])
            la.op("sp", stv, waits=W(("E1", v_ev[-1])),
                  inc=f"STV{v_n % 2}", amt=16, tag="vst")
            v_n += 1
        pair_p1_end[p] = sch.v("P1")

    ST1_ALL_W = [("STQ0", sch.v("STQ0")), ("STQ1", sch.v("STQ1")),
                 ("STQ2", sch.v("STQ2")), ("STQ3", sch.v("STQ3")),
                 ("STV0", sch.v("STV0")), ("STV1", sch.v("STV1"))]
    E1_ALL = sch.v("E1")
    P1_ALL = sch.v("P1")

    if 2 not in phases:
        def dbg(gq):
            return gq.dma_start(yT[:, 0:2048],
                                q_s[:, :].rearrange("p (a b) -> p a b",
                                                    a=2)[:, 0, :])
        la.op("gp", dbg, waits=W(*ST1_ALL_W), inc="OS0", amt=16)
        def dbg2(gq):
            return gq.dma_start(yT[:, 2048:4096],
                                k_s[:, :].rearrange("p (a b) -> p a b",
                                                    a=2)[:, 0, :])
        la.op("gp", dbg2, waits=W(*ST1_ALL_W), inc="OS0", amt=16)
        la.op("sp", lambda s_: s_.nop(), waits=W(("OS0", sch.v("OS0"))))
        sch.run_merge()
        return _finish(nc, sch, sems, st)

    for ec in range(NE):
        def ldwo(gq, ec=ec):
            return gq.dma_start(wout_sb[:, ec * 1024:(ec + 1) * 1024],
                                w_outT[ec * 128:(ec + 1) * 128, :])
        la.op("gp", ldwo, waits=W(("P1", P1_ALL)), inc="LWO", amt=16,
              tag="ldwo")
    LW_WOUT = sch.v("LWO")

    ln_state = {"out_n": 0}
    zglob = []
    cs_ready = [None] * NT
    proj_done = [None] * NT

    zstate = {"wide": False, "last": [None, None], "rr": 0,
              "first": True}

    def z7use():
        if not zstate["wide"]:
            w = zone7[0]
            return [w] if w is not None else [], Z7
        b = zstate["rr"]
        zstate["rr"] ^= 1
        zstate["cur"] = b
        w = zstate["last"][b]
        ws = [w] if w is not None else []
        if zstate["first"]:
            ws = ws + W(("EX", 4096))
            if b == 1:
                zstate["first"] = False
        return ws, b * 512

    def z7done(marker):
        if not zstate["wide"]:
            zone7[0] = marker
        else:
            zstate["last"][zstate["cur"]] = marker

    def ln_rows_b(v, zc_):
        m = rows3[0:1, 0:512]
        var = rows3[0:1, 512:1024]
        y = rows3[0:1, 1024:1536]
        t1 = mb_sb[0:1, :]
        t2 = rb3_sb[0:1, :]
        v.tensor_scalar(t1, P[0:1, zc_:zc_ + 512], 1.0 / E, None,
                        ALU.mult)
        v.tensor_tensor(t2, m, m, ALU.mult)
        v.tensor_tensor(var, t1, t2, ALU.subtract)
        v.tensor_scalar(var, var, EPS, None, ALU.add)
        v.tensor_scalar(t1, var, RC3, RC2, ALU.mult, ALU.add)
        v.tensor_tensor(t2, t1, var, ALU.mult)
        v.tensor_scalar(t1, t2, RC1, None, ALU.add)
        v.tensor_tensor(t2, t1, var, ALU.mult)
        v.tensor_scalar(y, t2, RC0, None, ALU.add)
        v.tensor_scalar(y, y, 0.02, None, ALU.max)
        for _ in range(4):
            v.tensor_tensor(t1, y, y, ALU.mult)
            v.tensor_tensor(t2, t1, var, ALU.mult)
            v.tensor_scalar(t1, t2, -0.5, 1.5, ALU.mult, ALU.add)
            v.tensor_tensor(y, y, t1, ALU.mult)
        v.tensor_copy(rows3b[0:1, 0:512], y)
        v.tensor_tensor(rows3b[0:1, 512:1024], y,
                        rows3b[0:1, 0:512], ALU.subtract)
        return v.tensor_copy(rows3b[0:1, 1024:1536], m)

    def collect_post(it):
        if it == NT - 1:
            zstate["wide"] = True
        itp = 0
        ctw = list(cs_ready[it])
        if it >= 1:
            ctw.append(("P3", proj_done[it - 1]))
        def ldc(s, it=it):
            src = ctx_s[:, it * 512:(it + 1) * 512].rearrange(
                "(c p) i -> p c i", p=128)
            return s.dma_start(ctx_t[:, :], src)
        lp.op("sp", ldc, waits=W(*ctw), inc="LC", amt=16, tag="ldc")
        l3c = sch.v("LC")

        zrec = []
        for fc in range(NE):
            gidx = it * NE + fc
            xw = []
            if gidx >= 2:
                xw.append(("V3", zglob[gidx - 2]))
            def ldx3(s, it=it, fc=fc):
                return s.dma_start(
                    x_ring[:, (fc % 2) * 512:(fc % 2) * 512 + 512],
                    xT[fc * 128:(fc + 1) * 128,
                       it * 512:(it + 1) * 512])
            lp.op("sp", ldx3, waits=W(*xw), inc=f"LX{fc % 2}", amt=16,
                  tag="ldxr")
            x_rec_fc = (f"LX{fc % 2}", 16 * (gidx // 2 + 1))

            zw_, zc_ = z7use()
            pw = W(("LWO", LW_WOUT), ("LC", l3c)) + zw_
            def pmm(t, fc=fc, zc_=zc_):
                last = None
                for ec in range(NE):
                    last = t.matmul(
                        P[:, zc_:zc_ + 512],
                        wout_sb[:, ec * 1024 + fc * 128:
                                ec * 1024 + fc * 128 + 128],
                        ctx_t[:, ec * 512:ec * 512 + 512],
                        start=(ec == 0), stop=(ec == NE - 1))
                return last
            lp.op("pe", pmm, waits=pw, inc="P3", tag="proj")
            p3v = sch.v("P3")
            def zop(v, fc=fc, zc_=zc_):
                return v.scalar_tensor_tensor(
                    z[:, fc * 512:(fc + 1) * 512],
                    P[:, zc_:zc_ + 512],
                    bo[:, fc:fc + 1],
                    x_ring[:, (fc % 2) * 512:(fc % 2) * 512 + 512],
                    ALU.add, ALU.add)
            zv = lp.op("dve", zop, waits=W(("P3", p3v), x_rec_fc),
                       inc="V3", tag="z")
            z7done((sems["V3"], zv))
            zrec.append(zv)
            zglob.append(zv)
        proj_done[it] = sch.v("P3")

        def ln_phase(zt, zwait, store_out):
            for fc in range(NE):
                def sqop(v, fc=fc, zt=zt):
                    return v.tensor_tensor(
                        sq8[:, fc * 512:(fc + 1) * 512],
                        zt[:, fc * 512:(fc + 1) * 512],
                        zt[:, fc * 512:(fc + 1) * 512], ALU.mult)
                lp.op("dve", sqop, waits=W(("V3", zwait[fc])), inc="V3",
                      tag="sq")
            sqv = sch.v("V3")

            zw_, zc_ = z7use()
            def s1g(t, zt=zt, zc_=zc_):
                last = None
                for fc in range(NE):
                    last = t.matmul(P[0:1, zc_:zc_ + 512], ones_f32[:, :],
                                    zt[:, fc * 512:(fc + 1) * 512],
                                    start=(fc == 0), stop=(fc == NE - 1),
                                    skip_group_check=True)
                return last
            lp.op("pe", s1g, waits=W(("V3", zwait[-1])) + zw_,
                  inc="P3", tag="s1")
            s1v = sch.v("P3")
            def lra(v, zc_=zc_):
                return v.tensor_scalar(rows3[0:1, 0:512],
                                       P[0:1, zc_:zc_ + 512],
                                       1.0 / E, None, ALU.mult)
            mv = lp.op("dve", lra, waits=W(("P3", s1v)), inc="V3",
                       tag="rowsa")
            z7done((sems["V3"], mv))

            zw_, zc_ = z7use()
            def s2g(t, zc_=zc_):
                last = None
                for fc in range(NE):
                    last = t.matmul(P[0:1, zc_:zc_ + 512], ones_bf[:, 0:1],
                                    sq8[:, fc * 512:(fc + 1) * 512],
                                    start=(fc == 0), stop=(fc == NE - 1),
                                    skip_group_check=True)
                return last
            lp.op("pe", s2g, waits=W(("V3", sqv)) + zw_, inc="P3",
                  tag="s2")
            s2v = sch.v("P3")
            rv = lp.op("dve",
                       (lambda v, zc_=zc_: ln_rows_b(v, zc_)),
                       waits=W(("P3", s2v)), inc="V3", tag="rowsb")
            z7done((sems["V3"], rv))

            zw_, zc_ = z7use()
            def mbg(t, zc_=zc_):
                return t.matmul(P[:, zc_:zc_ + 512], ones_bf[0:1, :],
                                rows3b[0:1, 1024:1536], start=True,
                                stop=True, tile_position=(0, 0),
                                skip_group_check=True)
            lp.op("pe", mbg, waits=W(("V3", rv)) + zw_, inc="P3",
                  tag="mb")
            mbv = sch.v("P3")
            def mbe(v, zc_=zc_):
                return v.tensor_copy(mb_sb[:, :], P[:, zc_:zc_ + 512])
            mbev = lp.op("dve", mbe, waits=W(("P3", mbv)), inc="V3",
                         tag="mbe")
            z7done((sems["V3"], mbev))

            zw_, zc_ = z7use()
            def rsg(t, zc_=zc_):
                t.matmul(P[:, zc_:zc_ + 512], ones_bf[0:1, :],
                         rows3b[0:1, 0:512], start=True, stop=False,
                         tile_position=(0, 0), skip_group_check=True)
                return t.matmul(P[:, zc_:zc_ + 512], ones_bf[0:1, :],
                                rows3b[0:1, 512:1024], start=False,
                                stop=True, tile_position=(0, 0),
                                skip_group_check=True)
            lp.op("pe", rsg, waits=W(("V3", rv)) + zw_, inc="P3",
                  tag="rs")
            rsv = sch.v("P3")
            def rse(v, zc_=zc_):
                return v.tensor_copy(rb3_sb[:, :], P[:, zc_:zc_ + 512])
            rsev = lp.op("dve", rse, waits=W(("P3", rsv)), inc="V3",
                         tag="rse")
            z7done((sems["V3"], rsev))
            bevv = rsev

            outrec = []
            for fc in range(NE):
                if not store_out:
                    def nm(v, fc=fc, zt=zt):
                        v.tensor_tensor(sq[:, :],
                                        zt[:, fc * 512:(fc + 1) * 512],
                                        mb_sb[:, :], ALU.subtract)
                        v.tensor_tensor(y1f[:, fc * 512:(fc + 1) * 512],
                                        sq[:, :], rb3_sb[:, :], ALU.mult)
                        return v.tensor_copy(
                            y1b[:, fc * 512:(fc + 1) * 512],
                            y1f[:, fc * 512:(fc + 1) * 512])
                    lp.op("dve", nm, waits=W(("V3", bevv)), inc="V3",
                          tag="y1")
                    outrec.append(sch.v("V3"))
                else:
                    out_n = ln_state["out_n"]
                    ow = [("V3", bevv)]
                    if out_n >= 2:
                        ow.append((f"OS{out_n % 2}", 16 * (out_n // 2)))
                    def nm(v, fc=fc, zt=zt, out_n=out_n):
                        v.tensor_tensor(sq[:, :],
                                        zt[:, fc * 512:(fc + 1) * 512],
                                        mb_sb[:, :], ALU.subtract)
                        return v.tensor_tensor(
                            out_t[:, (out_n % 2) * 512:
                                  (out_n % 2) * 512 + 512],
                            sq[:, :], rb3_sb[:, :], ALU.mult)
                    ov = lp.op("dve", nm, waits=W(*ow), inc="V3",
                               tag="out")
                    def sto(s, fc=fc, it=it, out_n=out_n):
                        return s.dma_start(
                            yT[fc * 128:(fc + 1) * 128,
                               it * 512:(it + 1) * 512],
                            out_t[:, (out_n % 2) * 512:
                                  (out_n % 2) * 512 + 512])
                    lp.op("sp", sto, waits=W(("V3", ov)),
                          inc=f"OS{out_n % 2}", amt=16, tag="sto")
                    ln_state["out_n"] += 1
                    outrec.append(ov)
            return outrec

        y1rec = ln_phase(z, zrec, False)
        y1_all = y1rec[-1]

        p3now = sch.v("P3")
        for sl, wt in ((0, w_ffn), (1, w_a1)):
            for ec in range(NE):
                def ldw3(gq, sl=sl, wt=wt, ec=ec):
                    return gq.dma_start(
                        wslot[sl][:, ec * 1024:(ec + 1) * 1024],
                        wt[ec * 128:(ec + 1) * 128, :])
                lp.op("gp", ldw3, waits=W(("P3", p3now)),
                      inc=f"LW{sl}", amt=16, tag="ldwf")
        lw_w = ("LW0", sch.v("LW0"))
        lw_a1 = ("LW1", sch.v("LW1"))

        def ffn_stage(rhs, wsl, lww, inwait, evac_dst, tagp):
            recs = []
            for dc in range(NE):
                zw_, zc_ = z7use()
                hw = W(lww, ("V3", inwait)) + zw_
                def hmm(t, dc=dc, rhs=rhs, wsl=wsl, zc_=zc_):
                    last = None
                    for ec in range(NE):
                        last = t.matmul(
                            P[:, zc_:zc_ + 512],
                            wslot[wsl][:, ec * 1024 + dc * 128:
                                       ec * 1024 + dc * 128 + 128],
                            rhs[:, ec * 512:ec * 512 + 512],
                            start=(ec == 0), stop=(ec == NE - 1))
                    return last
                lp.op("pe", hmm, waits=hw, inc="P3", tag=tagp)
                pv3 = sch.v("P3")
                def hev(v, dc=dc, evac_dst=evac_dst, zc_=zc_):
                    return v.tensor_copy(
                        evac_dst[:, dc * 512:(dc + 1) * 512],
                        P[:, zc_:zc_ + 512])
                hv = lp.op("dve", hev, waits=W(("P3", pv3)), inc="V3",
                           tag=tagp + "e")
                z7done((sems["V3"], hv))
                recs.append(hv)
            return recs

        h1rec = ffn_stage(y1b, 0, lw_w, y1_all, h1b, "h1")
        h1_last_p3 = sch.v("P3")
        for ec in range(NE):
            def lda2(gq, ec=ec):
                return gq.dma_start(wslot[0][:, ec * 1024:(ec + 1) * 1024],
                                    w_a2[ec * 128:(ec + 1) * 128, :])
            lp.op("gp", lda2, waits=W(("P3", h1_last_p3)), inc="LW0",
                  amt=16, tag="lda")
        lw_a2 = ("LW0", sch.v("LW0"))
        h2rec = ffn_stage(h1b, 1, lw_a1, h1rec[-1], h2b, "h2")

        z2rec = []
        for dc in range(NE):
            zw_, zc_ = z7use()
            hw = W(lw_a2, ("V3", h2rec[-1])) + zw_
            def h3mm(t, dc=dc, zc_=zc_):
                last = None
                for ec in range(NE):
                    last = t.matmul(
                        P[:, zc_:zc_ + 512],
                        wslot[0][:, ec * 1024 + dc * 128:
                                 ec * 1024 + dc * 128 + 128],
                        h2b[:, ec * 512:ec * 512 + 512],
                        start=(ec == 0), stop=(ec == NE - 1))
                return last
            lp.op("pe", h3mm, waits=hw, inc="P3", tag="h3")
            pv3 = sch.v("P3")
            aw = [("P3", pv3)]
            if dc >= 2:
                aw.append(("V3", z2rec[dc - 2]))
            def th(a, dc=dc, zc_=zc_):
                return a.activation(
                    ffn_t[:, (dc % 2) * 512:(dc % 2) * 512 + 512],
                    P[:, zc_:zc_ + 512],
                    AF.Tanh, bias=bf[:, dc:dc + 1], scale=1.0)
            a3v = lp.op("act", th, waits=W(*aw), inc="A3", tag="tanh")
            z7done((sems["A3"], a3v))
            def z2op(v, dc=dc):
                return v.tensor_tensor(
                    z[:, dc * 512:(dc + 1) * 512],
                    y1f[:, dc * 512:(dc + 1) * 512],
                    ffn_t[:, (dc % 2) * 512:(dc % 2) * 512 + 512],
                    ALU.add)
            z2rec.append(lp.op("dve", z2op, waits=W(("A3", a3v)),
                               inc="V3", tag="z2"))

        ln_phase(z, z2rec, True)

    g = 0
    u = 0
    mk_count = 0
    last_st_w = {}

    for it in range(NT):
        for p in range(NP):
            lw = list(ST1_ALL_W)
            if u >= 2:
                lw.append(("SC", 32 * (u - 1)))
            def ldk(s, p=p, u=u):
                return s.dma_start(
                    k_pair[:, (u % 2) * S:(u % 2) * S + S],
                    k_s[p * 128:(p + 1) * 128, :])
            la.op("sp", ldk, waits=W(*lw), inc=f"KVK{u % 2}", amt=16,
                  tag="ldk")
            kvk_rdy = (f"KVK{u % 2}", 16 * (u // 2 + 1))
            vw = list(ST1_ALL_W)
            if u >= 1:
                vw.append(("CX", 4 * u))
            for h in range(2):
                def ldv(s, p=p, h=h):
                    dst = v_pair[:, :].rearrange(
                        "p (j c) -> p j c",
                        c=130)[:, :, 65 * h:65 * h + 64]
                    src = v_s[p, :, :].rearrange(
                        "(c p) d -> p c d",
                        p=128)[:, :, 64 * h:64 * h + 64]
                    return s.dma_start(dst, src)
                la.op("sp", ldv, waits=W(*vw), inc="KVV", amt=16,
                      tag="ldv")
            kvv_rdy = ("KVV", 32 * (u + 1))
            qw = list(ST1_ALL_W)
            if u >= 2:
                qw.append(("SC", 32 * (u - 1)))
            def ldq(s, p=p, it=it, u=u):
                return s.dma_start(
                    q_buf[:, (u % 2) * 512:(u % 2) * 512 + 512],
                    q_s[p * 128:(p + 1) * 128, it * 512:(it + 1) * 512])
            la.op("sp", ldq, waits=W(*qw), inc=f"KVQ{u % 2}", amt=16,
                  tag="ldq")
            kvq_rdy = (f"KVQ{u % 2}", 16 * (u // 2 + 1))

            def ctx_half(b, half, u=u, kvv_rdy=kvv_rdy):
                cwaits = [("EX", 2 * (u * 32 + b * BST + BST)),
                          ("E1", E1_ALL), kvv_rdy]
                if b == 0:
                    cwaits.append(("ACC", u))
                pc = 2048 + half * 512
                vo = 65 * half
                def cgr(t, b=b, u=u, pc=pc, vo=vo, half=half):
                    last = None
                    for sub in range(BST):
                        jc = b * BST + sub
                        eb = ((u * 32 + jc) % 14) * 1024 + 512 * half
                        last = t.matmul(
                            P[0:65, pc:pc + 512],
                            v_pair[:, jc * 130 + vo:jc * 130 + vo + 65],
                            E_sb[:, eb:eb + 512],
                            start=(b == 0 and sub == 0),
                            stop=(b == NB - 1 and sub == BST - 1),
                            skip_group_check=True)
                    return last
                la.op("pe", cgr, waits=W(*cwaits),
                      inc=("CX" if half == 1 else None), tag="cx")
                if half == 0 or b != NB - 1:
                    return
                cxv = sch.v("CX")
                def accop(v, u=u):
                    slot = (u % 2) * 1024
                    return v.tensor_copy(
                        acc[0:65, slot:slot + 1024],
                        P[0:65, 2048:3072])
                la.op("dve", accop, waits=W(("CX", cxv)), inc="ACC",
                      tag="acc")

            for b in range(NB):
                for sub in range(BST):
                    jc = b * BST + sub
                    cA = 512 * SBANKS[(2 * g) % 5]
                    cB = 512 * SBANKS[(2 * g + 1) % 5]
                    swaits = [kvk_rdy, kvq_rdy, ("E1", E1_ALL)]
                    if g >= 2:
                        swaits.append(("EX", 2 * g - 3))
                    def smm(t, jc=jc, cA=cA, cB=cB, u=u):
                        t.matmul(
                            P[:, cA:cA + 512],
                            k_pair[0:64, (u % 2) * S + jc * 128:
                                   (u % 2) * S + jc * 128 + 128],
                            q_buf[0:64,
                                  (u % 2) * 512:(u % 2) * 512 + 512],
                            start=True, stop=True, tile_position=(0, 0))
                        return t.matmul(
                            P[:, cB:cB + 512],
                            k_pair[64:128, (u % 2) * S + jc * 128:
                                   (u % 2) * S + jc * 128 + 128],
                            q_buf[64:128,
                                  (u % 2) * 512:(u % 2) * 512 + 512],
                            start=True, stop=True, tile_position=(64, 0))
                    la.op("pe", smm, waits=W(*swaits), inc="SC",
                          tag="sc")
                    scv = sch.v("SC")

                    for (dst, src, wd) in mask_windows(it, jc):
                        def madd(v, dst=dst, src=src, wd=wd, cA=cA,
                                 cB=cB):
                            v.tensor_tensor(
                                P[:, cA + dst:cA + dst + wd],
                                P[:, cA + dst:cA + dst + wd],
                                maskT[:, src:src + wd], ALU.add)
                            return v.tensor_tensor(
                                P[:, cB + dst:cB + dst + wd],
                                P[:, cB + dst:cB + dst + wd],
                                maskT[:, src:src + wd], ALU.add)
                        la.op("dve", madd,
                              waits=W(("SC", scv), ("LDB", LDB_ALL)),
                              inc="MK", tag="mk")
                        mk_count += 1

                    ewaits = [("SC", scv), ("MK", mk_count)]
                    if g >= 14:
                        ewaits.append(("CX", (g - 14) // 8 + 1))
                    if cB == cA + 512:
                        def expF(a, g=g, cA=cA):
                            return a.activation(
                                E_sb[:, (g % 14) * 1024:
                                     (g % 14) * 1024 + 1024],
                                P[:, cA:cA + 1024],
                                AF.Exp, bias=cexp[:, :], scale=0.125)
                        la.op("act", expF, waits=W(*ewaits), inc="EX",
                              amt=2, tag="expf")
                    else:
                        def expA(a, g=g, cA=cA):
                            return a.activation(
                                E_sb[:, (g % 14) * 1024:
                                     (g % 14) * 1024 + 512],
                                P[:, cA:cA + 512],
                                AF.Exp, bias=cexp[:, :], scale=0.125)
                        la.op("act", expA, waits=W(*ewaits), inc="EX",
                              tag="exp")
                        def expB(a, g=g, cB=cB):
                            return a.activation(
                                E_sb[:, (g % 14) * 1024 + 512:
                                     (g % 14) * 1024 + 1024],
                                P[:, cB:cB + 512],
                                AF.Exp, bias=cexp[:, :], scale=0.125)
                        la.op("act", expB, waits=W(), inc="EX",
                              tag="exp")
                    g += 1
                    if b >= 1 and sub == 2:
                        ctx_half(b - 1, 0)
                    if b >= 1 and sub == 5:
                        ctx_half(b - 1, 1)
            ctx_half(NB - 1, 0)
            ctx_half(NB - 1, 1)

            accv = sch.v("ACC")
            def st1_(v, u=u):
                slot = (u % 2) * 1024
                return v.reciprocal(rows3[64:65, 0:1024],
                                    acc[64:65, slot:slot + 1024])
            n1av = lnm.op("dve", st1_, W(("ACC", accv)), inc="N1A",
                          tag="nq_st1")
            def st1d(sq_, u=u):
                return sq_.dma_start(den_d[u % 2:u % 2 + 1, :],
                                     rows3[64:65, 0:1024])
            dnw = [("N1A", n1av)]
            if u >= 1:
                dnw.append(("BC", 16 * u))
            n1v = lnm.op("sp", st1d, W(*dnw), inc="N1", amt=16,
                         tag="dnst")
            def st1b(sq_, u=u):
                dst2, src2 = broadcast_tensor_aps(
                    den_bc[0:64, :], den_d[u % 2:u % 2 + 1, :])
                return sq_.dma_start(dst2, src2)
            bcw = [("N1", n1v)]
            if u >= 1:
                bcw.append(("N2", 2 * u))
            bcv = lnm.op("sp", st1b, W(*bcw), inc="BC", amt=16,
                         tag="dnbc")

            def st3a(v, u=u):
                slot = (u % 2) * 1024
                return v.tensor_tensor(Rb_sb[0:64, 0:512],
                                       acc[0:64, slot:slot + 512],
                                       den_bc[0:64, 0:512], ALU.mult)
            s3w = [("BC", bcv)]
            if u >= 1:
                s3w.append(("N3", u))
            lnm.op("dve", st3a, W(*s3w), inc="N2", tag="nq_st3")

            def st3b(v, u=u):
                slot = (u % 2) * 1024
                return v.tensor_tensor(
                    Rb_sb[0:64, 512:1024],
                    acc[0:64, slot + 512:slot + 1024],
                    den_bc[0:64, 512:1024], ALU.mult)
            lnm.op("dve", st3b, W(), inc="N2", tag="nq_st3")

            st4w = []
            if u >= 1 and (u - 1) in last_st_w:
                st4w = last_st_w[u - 1]
            def st4_(v, p=p, u=u):
                v.tensor_scalar(stage_sb[0:64, 0:512],
                                Rb_sb[0:64, 0:512], bv[0:64, p:p + 1],
                                None, ALU.add)
                return v.tensor_scalar(
                    stage_sb[0:64, 512:1024],
                    Rb_sb[0:64, 512:1024],
                    bvB[0:64, p:p + 1], None, ALU.add)
            n3v = lnm.op("dve", st4_, W(("N2", 2 * u + 2), *st4w),
                         inc="N3", tag="nq_st4")

            for half in range(2):
                def stc(s, p=p, it=it, half=half):
                    so = half * 512
                    return s.dma_start(
                        ctx_s[p * 128 + half * 64:
                              p * 128 + half * 64 + 64,
                              it * 512:(it + 1) * 512],
                        stage_sb[0:64, so:so + 512])
                ls.op("sp", stc, W(("N3", n3v)), inc=f"CS{u % 2}",
                      amt=16, tag="stc")
            last_st_w[u] = [(f"CS{u % 2}", sch.v(f"CS{u % 2}"))]
            u += 1

        cs_ready[it] = [("CS0", sch.v("CS0")), ("CS1", sch.v("CS1"))]
        if 3 in phases:
            collect_post(it)

    if 3 not in phases:
        CS_ALL_W = [("CS0", sch.v("CS0")), ("CS1", sch.v("CS1"))]
        def dbg3(gq):
            return gq.dma_start(yT[:, :], ctx_s[:, :])
        la.op("gp", dbg3, waits=W(*CS_ALL_W), inc="OS0", amt=16)
        la.op("sp", lambda s_: s_.nop(), waits=W(("OS0", sch.v("OS0"))))
        sch.run_merge()
        return _finish(nc, sch, sems, st)

    la.op("sp", lambda s: s.nop(),
          waits=W(("OS0", sch.v("OS0")), ("OS1", sch.v("OS1"))),
          tag="fin")
    sch.run_merge()
    return _finish(nc, sch, sems, st)


def _finish(nc, sch, sems, st):
    global LAST_SCHED
    LAST_SCHED = sch

    def section(eng_obj, eng_name):
        waited = {}
        for waits, fn, inc, amt, tag in sch.ops[eng_name]:
            for (sem, val) in waits:
                key = id(sem)
                if waited.get(key, -1) < val:
                    eng_obj.wait_ge(sem, val)
                    waited[key] = val
            inst = fn(eng_obj)
            if inc is not None:
                inst.then_inc(sems[inc], amt)

    with nc.Block() as block:
        @block.tensor
        def _(t):
            section(t, "pe")

        @block.scalar
        def _(a):
            section(a, "act")

        @block.vector
        def _(v):
            section(v, "dve")

        @block.gpsimd
        def _(gq):
            section(gq, "gp")

        @block.sync
        def _(s):
            section(s, "sp")

    st.close()
    return nc


_NC_CACHE = None


def kernel(L0_attn_input_embeddings, L0_attn_token_ids,
           L0_attn_in_proj_weight, L0_attn_in_proj_bias,
           L0_attn_out_proj_weight, L0_attn_out_proj_bias,
           L0_ffn_weight, L0_ffn_aux_weight1, L0_ffn_aux_weight2,
           L0_ffn_bias):
    global _NC_CACHE
    _ = L0_attn_token_ids
    x = np.asarray(L0_attn_input_embeddings, dtype=np.float32)
    w_in = np.asarray(L0_attn_in_proj_weight, dtype=np.float32)
    b_in = np.asarray(L0_attn_in_proj_bias, dtype=np.float32)
    w_out = np.asarray(L0_attn_out_proj_weight, dtype=np.float32)
    b_out = np.asarray(L0_attn_out_proj_bias, dtype=np.float32)
    w_f = np.asarray(L0_ffn_weight, dtype=np.float32)
    a1 = np.asarray(L0_ffn_aux_weight1, dtype=np.float32)
    a2 = np.asarray(L0_ffn_aux_weight2, dtype=np.float32)
    b_f = np.asarray(L0_ffn_bias, dtype=np.float32)

    w_inT = np.ascontiguousarray(w_in.T)
    w_outT = np.ascontiguousarray(w_out.T)
    bq = np.ascontiguousarray(b_in[0:E].reshape(NE, 128).T)
    bk = np.ascontiguousarray(b_in[E:2 * E].reshape(NE, 128).T)
    bvv = np.ascontiguousarray(b_in[2 * E:3 * E].reshape(NE, 128).T)
    bvB = np.ascontiguousarray(np.roll(b_in[2 * E:3 * E].reshape(NE, 128).T,
                                       -64, axis=0))
    bo = np.ascontiguousarray(b_out.reshape(NE, 128).T)
    bff = np.ascontiguousarray(b_f.reshape(NE, 128).T)
    pp, cc = np.arange(128)[:, None], np.arange(144)[None, :]
    import ml_dtypes
    maskT = np.where((cc - pp >= 0) & (cc - pp <= WINDOW), NEG,
                     0.0).astype(ml_dtypes.bfloat16)

    if _NC_CACHE is None:
        _NC_CACHE = build_nc()
    nc = _NC_CACHE

    in_maps = []
    for c in range(8):
        in_maps.append({
            "xT": np.ascontiguousarray(x[c].T),
            "w_inT": w_inT, "w_outT": w_outT,
            "w_ffn": w_f, "w_a1": a1, "w_a2": a2,
            "bq": bq, "bk": bk, "bv": bvv, "bvB": bvB, "bo": bo,
            "bf": bff, "maskT": maskT,
        })
    res = run_bass_kernel_spmd(nc, in_maps, core_ids=list(range(8)))
    out = np.stack([np.ascontiguousarray(r["yT"].T) for r in res.results])
    return out.astype(np.float32)
